# version 11
# baseline (speedup 1.0000x reference)
import os as _os
import numpy as np
import concourse.bass as bass
import concourse.mybir as mybir
from concourse.bass_utils import run_bass_kernel_spmd
from concourse.alu_op_type import AluOpType as OP

F32 = mybir.dt.float32
BF16 = mybir.dt.bfloat16
AF = mybir.ActivationFunctionType
AX = mybir.AxisListType

T = int(_os.environ.get("K_T", "2048"))
D = 2048
DI = 4096
DFF = 5504
TC = min(512, T)
NTC = T // TC
TC2 = min(1024, T)
NT2 = T // TC2
EPS = 1e-5
NCORES = 8

ENGS = ("sync", "tensor", "vector", "scalar", "gpsimd")


def spans(total, step=512):
    return [(i, min(step, total - i)) for i in range(0, total, step)]


class Sem:
    def __init__(self, handle):
        self.h = handle
        self.n = 0


class Arena:

    def __init__(self, nc, base, top):
        self.nc, self.base, self.top = nc, base, top
        self.off = 0
        self.cnt = 0

    def reset(self):
        self.off = 0

    def mark(self):
        return self.off

    def rewind(self, m):
        self.off = m

    def _alloc(self, n, dtype, bsz):
        off = (self.base + self.off + 31) & ~31
        h = self.nc.alloc_sbuf_tensor_at(f"arn{self.cnt}", [128, n], dtype,
                                         offset=off)
        self.cnt += 1
        self.off = off + n * bsz - self.base
        assert self.base + self.off <= self.top, \
            f"arena overflow: {self.base + self.off} > {self.top}"
        return h[:, :]

    def f32(self, n):
        return self._alloc(n, F32, 4)

    def bf16(self, n):
        return self._alloc(n, BF16, 2)


class Ctx:
    def __init__(self, nc, dram, sb, ps, sems, pool_handles, arena):
        self.nc, self.dram, self.sb, self.ps = nc, dram, sb, ps
        self.sems = sems
        self.ar = arena
        self.q = {e: [] for e in ENGS}
        self.pe, self.dv, self.ac, self.gp = (sems["pe"], sems["dv"],
                                              sems["ac"], sems["gp"])
        self._pool_handles = pool_handles
        self._pool = {}

    def s(self, name):
        if name not in self._pool:
            idx = len(self._pool)
            assert idx < len(self._pool_handles), "sem pool exhausted"
            self._pool[name] = Sem(self._pool_handles[idx])
        return self._pool[name]

    def _emit(self, eng, fn, sem, k):
        if sem is None:
            self.q[eng].append(fn)
            return None
        sem.n += k
        h = sem.h
        self.q[eng].append(lambda e, f=fn: f(e).then_inc(h, k))
        return sem.n

    @staticmethod
    def _pick(sem, default):
        if isinstance(sem, Sem):
            return sem
        return default if sem else None

    def set_filler(self, gen):
        self._filler = gen

    def run_filler(self, n=1):
        g = getattr(self, "_filler", None)
        if g is None:
            return
        for _ in range(n):
            try:
                next(g)
            except StopIteration:
                self._filler = None
                return

    def drain_filler(self):
        while getattr(self, "_filler", None) is not None:
            self.run_filler(64)

    def wait(self, eng, sem, val=None):
        if isinstance(sem, tuple):
            sem, val = sem
        if val:
            h = sem.h
            self.q[eng].append(lambda e: e.wait_ge(h, val))

    def dma(self, eng, out, in_, sem, slow=False):
        return self._emit(eng,
                          lambda e, o=out, i=in_, s=slow:
                          e.dma_start(out=o, in_=i, allow_slow_non_contiguous=s),
                          sem, 16)

    def mm(self, out, lhsT, rhs, start, stop, sem=True):
        return self._emit("tensor",
                          lambda e, o=out, l=lhsT, r=rhs, s0=start, s1=stop:
                          e.matmul(o, l, r, start=s0, stop=s1),
                          self._pick(sem, self.pe), 1)

    def transpose(self, out, in_, sem=True):
        ident = self.sb["ident"][:]
        return self._emit("tensor",
                          lambda e, o=out, i=in_, d=ident: e.transpose(o, i, d),
                          self._pick(sem, self.pe), 1)

    def act(self, out, in_, func, bias=0.0, scale=1.0, sem=True):
        return self._emit("scalar",
                          lambda e, o=out, i=in_, f=func, b=bias, s=scale:
                          e.activation(o, i, f, bias=b, scale=s),
                          self._pick(sem, self.ac), 1)

    def tt(self, out, a, b, op, sem=True, eng="vector"):
        sm = self._pick(sem, self.gp if eng == "gpsimd" else self.dv)
        return self._emit(eng,
                          lambda e, o=out, x=a, y=b, p=op: e.tensor_tensor(o, x, y, p),
                          sm, 1)

    def ts2(self, out, in_, s1, s2, op0, op1, sem=True):
        return self._emit("vector",
                          lambda e, o=out, i=in_, a=s1, b=s2, p0=op0, p1=op1:
                          e.tensor_scalar(o, i, a, b, p0, p1),
                          self._pick(sem, self.dv), 1)

    def tss(self, out, in_, s, op, sem=True):
        return self._emit("vector",
                          lambda e, o=out, i=in_, a=s, p=op:
                          e.tensor_single_scalar(o, i, a, p),
                          self._pick(sem, self.dv), 1)

    def stt(self, out, in0, scalar, in1, op0, op1, sem=True):
        return self._emit("vector",
                          lambda e, o=out, i=in0, s=scalar, j=in1, p0=op0, p1=op1:
                          e.scalar_tensor_tensor(o, i, s, j, p0, p1),
                          self._pick(sem, self.dv), 1)

    def copy(self, out, in_, sem=True, eng="vector"):
        sm = self._pick(sem, self.gp if eng == "gpsimd" else self.dv)
        return self._emit(eng, lambda e, o=out, i=in_: e.tensor_copy(o, i), sm, 1)

    def scan(self, out, a, b, init, sem=True):
        return self._emit("vector",
                          lambda e, o=out, x=a, y=b, i=init:
                          e.tensor_tensor_scan(o, x, y, i, OP.mult, OP.add),
                          self.dv if sem else None, 1)

    def recip(self, out, in_, sem=True):
        fb = self.sb["rfence"][:, 0:1]
        v0 = self.copy(fb, in_[:, 0:1])
        self.wait("vector", self.dv, v0)
        v1 = self._emit("vector",
                        lambda e, o=out, i=in_: e.reciprocal(o, i), self.dv, 1)
        self.wait("vector", self.dv, v1)
        return v1

    def redmax(self, out, in_, sem=True):
        return self._emit("vector",
                          lambda e, o=out, i=in_:
                          e.tensor_reduce(o, i, AX.X, OP.max),
                          self.dv if sem else None, 1)

    def memset(self, ap, val, eng="vector", sem=True):
        sm = (self.gp if eng == "gpsimd" else self.dv) if sem else None
        return self._emit(eng, lambda e, a=ap, v=val: e.memset(a, v), sm, 1)

    def fence_loads(self):
        for eng in ("sync", "gpsimd"):
            for s in (self.pe, self.dv, self.ac, self.gp):
                self.wait(eng, s, s.n)

    def barrier(self, exclude=()):
        vals = [(s, s.n) for s in (self.pe, self.dv, self.ac, self.gp)]
        vals += [(s, s.n) for nm, s in self._pool.items()
                 if not any(nm.startswith(p) for p in exclude)]
        for eng in ENGS:
            for s, v in vals:
                self.wait(eng, s, v)

    def run(self, eng, e):
        for f in self.q[eng]:
            f(e)


def _is_dram(ap):
    try:
        return "DRam" in type(ap.tensor).__name__
    except Exception:
        return False


def _rows(ap, r0, n):
    return ap[r0:r0 + n, :]


def _host_layouts(inputs):
    g = {}

    def img(wT, nMo, ktiles, mcols=128):
        K, O = wT.shape
        assert K == 128 * ktiles and O == mcols * nMo
        a = wT.reshape(ktiles, 128, nMo, mcols).transpose(2, 1, 0, 3)
        return np.ascontiguousarray(a.reshape(nMo, 128, ktiles * mcols))

    f32 = lambda a: np.ascontiguousarray(np.asarray(a), dtype=np.float32)

    g["wt_inproj"] = img(f32(inputs["L0_attn_in_proj_w"]).T, 64, 16)
    g["wt_ffproj"] = img(f32(inputs["L0_attn_ff_proj_w"]).T, 64, 16)
    g["wt_outproj"] = img(f32(inputs["L0_attn_out_proj_w"]).T, 16, 32)
    g["wt_ffout"] = img(f32(inputs["L0_attn_ff_out_w"]).T, 16, 32)
    g["wt_gate"] = img(f32(inputs["L0_ffn_w_gate"]).T, 43, 16)
    g["wt_up"] = img(f32(inputs["L0_ffn_w_up"]).T, 43, 16)
    g["wt_down"] = img(f32(inputs["L0_ffn_w_down"]).T, 16, 43)
    xp = f32(inputs["L0_attn_x_proj_w"]).T
    g["wt_xproj"] = np.ascontiguousarray(
        xp.reshape(32, 128, 160).transpose(1, 0, 2).reshape(128, 32 * 160))
    g["wt_dtproj"] = f32(inputs["L0_attn_dt_proj_w"]).T.copy()

    def cols(v, n):
        return np.ascontiguousarray(f32(v).reshape(n, 128).T)

    g["convw_sb"] = np.ascontiguousarray(
        f32(inputs["L0_attn_conv_w"]).reshape(DI, 4).reshape(32, 128, 4)
        .transpose(1, 0, 2).reshape(128, 128))
    g["convb_sb"] = cols(inputs["L0_attn_conv_b"], 32)
    g["dtb_sb"] = cols(inputs["L0_attn_dt_proj_b"], 32)
    g["dsk_sb"] = cols(inputs["L0_attn_D_skip"], 32)
    g["ln1w_sb"] = cols(inputs["L0_attn_ln1_w"], 16)
    g["ln1b_sb"] = cols(inputs["L0_attn_ln1_b"], 16)
    g["ln2w_sb"] = cols(inputs["L0_attn_ln2_w"], 16)
    g["ln2b_sb"] = cols(inputs["L0_attn_ln2_b"], 16)
    al = f32(inputs["L0_attn_A_log"])
    g["alog_sb"] = np.ascontiguousarray(
        al.reshape(512, 8, 16).transpose(1, 2, 0).reshape(128, 512))
    g["ones_sb"] = np.ones((128, 128), dtype=np.float32)
    g["ident_sb"] = np.eye(128, dtype=np.float32)
    g["kconst_sb"] = np.tile(np.array([[EPS, -0.5, 1.0, 0.0]], dtype=np.float32),
                             (128, 1))
    psel = np.zeros((128, 16 * 128), dtype=np.float32)
    for kk in range(16):
        for p in range(128):
            psel[p, 128 * kk + 8 * kk + p // 16] = 1.0
    g["psel_sb"] = psel
    prep = np.zeros((128, 16 * 128), dtype=np.float32)
    for kk in range(16):
        for p in range(128):
            prep[8 * kk + p // 16, 128 * kk + p] = 1.0
    g["prep_sb"] = prep
    return g


def build_program(debug_taps=()):
    nc = bass.Bass("TRN2", target_bir_lowering=False,
                   detect_race_conditions=False)
    dram = {}

    def din(name, shape, dtype=F32):
        dram[name] = nc.dram_tensor(name, list(shape), dtype, kind="ExternalInput").ap()

    def dscr(name, shape, dtype=F32):
        kind = "Internal" if name not in debug_taps else "ExternalOutput"
        dram[name] = nc.dram_tensor(name, list(shape), dtype, kind=kind).ap()

    din("x_f", (D, T))
    din("wt_inproj", (64, 128, 2048)); din("wt_ffproj", (64, 128, 2048))
    din("wt_outproj", (16, 128, 4096)); din("wt_ffout", (16, 128, 4096))
    din("wt_gate", (43, 128, 2048)); din("wt_up", (43, 128, 2048))
    din("wt_down", (16, 128, 5504))
    din("wt_xproj", (128, 32 * 160)); din("wt_dtproj", (128, 4096))
    for nm, n in [("convw_sb", 128), ("convb_sb", 32), ("dtb_sb", 32),
                  ("dsk_sb", 32), ("ln1w_sb", 16), ("ln1b_sb", 16),
                  ("ln2w_sb", 16), ("ln2b_sb", 16), ("alog_sb", 512),
                  ("ones_sb", 128), ("ident_sb", 128), ("psel_sb", 16 * 128),
                  ("prep_sb", 16 * 128), ("kconst_sb", 4)]:
        din(nm, (128, n))
    dram["out_f"] = nc.dram_tensor("out_f", [D, T], F32, kind="ExternalOutput").ap()

    dscr("oph", (16, 128, 4096), BF16); dscr("opl", (16, 128, 4096), BF16)
    dscr("foh", (16, 128, 4096), BF16); dscr("fol", (16, 128, 4096), BF16)
    dscr("wqgh", (43, 128, 2048), BF16); dscr("wqgl", (43, 128, 2048), BF16)
    dscr("wquh", (43, 128, 2048), BF16); dscr("wqul", (43, 128, 2048), BF16)
    dscr("wqdh", (16, 128, 5504), BF16); dscr("wqdl", (16, 128, 5504), BF16)
    dscr("xnh", (D, T), BF16); dscr("xnl", (D, T), BF16)
    dscr("u1h", (D, T), BF16); dscr("u1l", (D, T), BF16)
    dscr("qb", (D, T), BF16)
    dscr("sflat", (1, T)); dscr("rflat", (1, T))
    dscr("xc", (DI, T)); dscr("sz", (DI, T)); dscr("dt", (DI, T)); dscr("uu", (DI, T))
    dscr("bcflat", (32, T))
    dscr("hid", (DFF, T))
    dscr("s2flat", (1, T)); dscr("r2flat", (1, T))
    dscr("hqb", (DFF, T), BF16)
    dscr("dwn", (D, T))
    dscr("ygh", (DI, T), BF16); dscr("ygl", (DI, T), BF16)
    dscr("anx", (D, T))
    dscr("ln2h", (D, T), BF16); dscr("ln2l", (D, T), BF16)
    dscr("hgh", (DI, T), BF16); dscr("hgl", (DI, T), BF16)

    import contextlib
    stack = contextlib.ExitStack()
    sb, ps = {}, {}

    def sbuf(name, shape, dtype=F32):
        sb[name] = stack.enter_context(nc.sbuf_tensor(name, list(shape), dtype))

    with stack:
        for nm, n in [("convw", 128), ("convb", 32), ("dtb", 32), ("dsk", 32),
                      ("ln1w", 16), ("ln1b", 16), ("ln2w", 16), ("ln2b", 16),
                      ("ones", 128), ("ident", 128), ("kconst", 4),
                      ("rfence", 8)]:
            sbuf(nm, (128, n))
        for ci, cv in enumerate([EPS, -0.5, 1.0, 0.0]):
            nc.const_aps.register(F32, cv, sb["kconst"][:, ci:ci + 1])
        sbuf("alog", (128, 512)); sbuf("aneg", (128, 512))
        sbuf("s1w", (128, 8 * 512)); sbuf("s1b", (128, 7 * 512), BF16)
        sbuf("s2b", (128, T)); sbuf("r2b", (128, T))
        for i in range(12):
            sbuf(f"wb{i}", (128, 512))
        for i in range(8):
            sbuf(f"bb{i}", (128, 512), BF16)
        sbuf("halo", (128, 3 * 32))
        for i in range(8):
            ps[f"pb{i}"] = stack.enter_context(
                nc.psum_tensor(f"pb{i}", [128, 512], F32))

        sems = {nm: Sem(stack.enter_context(nc.semaphore(nm)))
                for nm in ("pe", "dv", "ac", "gp")}
        pool_handles = [stack.enter_context(nc.semaphore(f"sp{i}"))
                        for i in range(96)]

        abase = (nc.sbuf_base + 31) & ~31
        c = Ctx(nc, dram, sb, ps, sems, pool_handles,
                Arena(nc, abase, nc.sbuf_top))

        emit_s0(c); c.barrier()
        c.set_filler(emit_s1(c))
        emit_s2(c); c.barrier(exclude=("s1",))
        emit_s3(c); c.barrier(exclude=("s1",))
        emit_s3b(c)
        c.drain_filler()
        c.barrier()
        emit_s4(c); c.barrier()
        c.set_filler(emit_s5(c))
        emit_s6(c)
        c.drain_filler()
        c.barrier()
        emit_s7(c); c.barrier()
        emit_s8(c); c.barrier()
        emit_s9(c); c.barrier()

        with nc.Block() as block:
            @block.sync
            def _(e):
                c.run("sync", e)

            @block.tensor
            def _(e):
                c.run("tensor", e)

            @block.vector
            def _(e):
                c.run("vector", e)

            @block.scalar
            def _(e):
                c.run("scalar", e)

            @block.gpsimd
            def _(e):
                c.run("gpsimd", e)

    return nc


def pair_split(c, src_f32, hi_bf, lo_bf, gsem):
    v = c.dma("gpsimd", hi_bf, src_f32, gsem)
    c.wait("vector", gsem, v)
    return c.tt(lo_bf, src_f32, hi_bf, OP.subtract)


def ladder_round_clip(c, v, t0, tbuf, dbuf, qout):
    c.ts2(v, v, -128.0, 127.0, OP.max, OP.min, sem=False)
    c.tss(t0, v, 128.5, OP.add, sem=False)
    c.copy(tbuf, t0, sem=False)
    for B in (128.0, 64.0, 32.0, 16.0, 8.0, 4.0, 2.0, 1.0):
        c.ts2(dbuf, tbuf, B, -B, OP.is_ge, OP.mult, sem=False)
        c.tt(tbuf, tbuf, dbuf, OP.add, sem=False)
    c.tt(t0, t0, tbuf, OP.subtract, sem=False)
    return c.tss(qout, t0, -128.0, OP.add)


def ln_block(c, src_buf, ntiles, tcw, pb_m, pb_q, sqb, wait_sem, wait_val,
             pe_sq):
    ps, sb = c.ps, c.sb
    c.wait("scalar", wait_sem, wait_val)
    c.wait("tensor", wait_sem, wait_val)
    for i in range(ntiles):
        v_sq = c.act(sqb[:, tcw * i:tcw * (i + 1)],
                     src_buf[:, tcw * i:tcw * (i + 1)], AF.Square)
        xi = src_buf[:, tcw * i:tcw * (i + 1)]
        for n0, nw in spans(tcw):
            c.mm(ps[pb_m][:, n0:n0 + nw], sb["ones"][:], xi[:, n0:n0 + nw],
                 start=(i == 0), stop=(i == ntiles - 1),
                 sem=(i == ntiles - 1))
        c.wait("tensor", c.ac, v_sq)
        for n0, nw in spans(tcw):
            c.mm(ps[pb_q][:, n0:n0 + nw], sb["ones"][:],
                 sqb[:, tcw * i:tcw * (i + 1)][:, n0:n0 + nw],
                 start=(i == 0), stop=(i == ntiles - 1),
                 sem=(i == ntiles - 1))
    return c.pe.n


def ln_finish(c, pb_m, pb_q, tcw, m_b, var, r_b, tmp):
    ps = c.ps
    c.wait("vector", c.pe, c.pe.n)
    c.tss(m_b, ps[pb_m][:, :tcw], 1.0 / D, OP.mult, sem=False)
    c.tss(var, ps[pb_q][:, :tcw], 1.0 / D, OP.mult, sem=False)
    c.tt(tmp, m_b, m_b, OP.mult, sem=False)
    v_dv = c.tt(var, var, tmp, OP.subtract)
    c.wait("scalar", c.dv, v_dv)
    c.act(tmp, var, AF.Ln, bias=EPS)
    v_ac = c.act(r_b, tmp, AF.Exp, scale=-0.5)
    c.wait("vector", c.ac, v_ac)


def emit_s0(c):
    sb, dram, ar = c.sb, c.dram, c.ar
    ar.reset()
    tgt = {"convw_sb": "convw", "convb_sb": "convb", "dtb_sb": "dtb",
           "dsk_sb": "dsk", "ln1w_sb": "ln1w", "ln1b_sb": "ln1b",
           "ln2w_sb": "ln2w", "ln2b_sb": "ln2b", "ones_sb": "ones",
           "ident_sb": "ident", "alog_sb": "alog",
           "kconst_sb": "kconst"}
    s_res = c.s("s0res")
    for src, dst in tgt.items():
        c.dma("sync", sb[dst][:], dram[src][:], s_res)
    v_res = s_res.n

    c.wait("scalar", s_res, v_res)
    c.act(sb["aneg"][:], sb["alog"][:], AF.Exp)
    v_exp = c.ac.n
    c.wait("vector", c.ac, v_exp)
    c.tss(sb["aneg"][:], sb["aneg"][:], -1.0, OP.mult)

    blk = [ar.f32(4096), ar.f32(4096)]
    blkh = [ar.bf16(4096), ar.bf16(4096)]
    blkl = [ar.bf16(4096), ar.bf16(4096)]
    s_ld = [c.s("s0ldA"), c.s("s0ldB")]
    s_g = [c.s("s0gA"), c.s("s0gB")]
    s_st = [c.s("s0stA"), c.s("s0stB")]
    st_done = {}
    it = 0
    for src, dh, dl in [("wt_outproj", "oph", "opl"), ("wt_ffout", "foh", "fol")]:
        for mo in range(16):
            r = it % 2
            if it >= 2:
                c.wait("sync", s_st[r], st_done[it - 2])
            v_ld = c.dma("sync", blk[r], dram[src][mo, :, :], s_ld[r])
            c.wait("gpsimd", s_ld[r], v_ld)
            if it >= 2:
                c.wait("gpsimd", s_st[r], st_done[it - 2])
            v_hi = c.dma("gpsimd", blkh[r], blk[r], s_g[r])
            c.wait("vector", s_g[r], v_hi)
            v_lo = c.tt(blkl[r], blk[r], blkh[r], OP.subtract)
            c.wait("sync", c.dv, v_lo)
            c.dma("sync", dram[dh][mo, :, :], blkh[r], s_st[r])
            c.dma("sync", dram[dl][mo, :, :], blkl[r], s_st[r])
            st_done[it] = s_st[r].n
            it += 1


def emit_s1(c):
    sb, ps, dram = c.sb, c.ps, c.dram
    limit = int(_os.environ.get("K_S1_LIMIT", "0")) or None
    cfgs = [("wt_gate", "wqgh", "wqgl", 43, 16),
            ("wt_up", "wquh", "wqul", 43, 16),
            ("wt_down", "wqdh", "wqdl", 16, 43)]
    s_ld = [c.s("s1ldA"), c.s("s1ldB")]
    s_st = [c.s("s1stA"), c.s("s1stB")]
    pe1, dv1, ac1 = c.s("s1pe"), c.s("s1dv"), c.s("s1ac")
    s_res = c.s("s0res")
    v_res = s_res.n
    first = True
    W = lambda i: sb["s1w"][:, 512 * i:512 * (i + 1)]
    B = lambda i: sb["s1b"][:, 512 * i:512 * (i + 1)]
    st_done, dv_done, pe_done = {}, {}, {}
    it = 0
    for srcn, dsth, dstl, nMo, nK in cfgs:
        for mo in range(nMo):
            for k4 in range(0, nK, 4):
                if limit is not None and it >= limit:
                    return
                cw = min(4, nK - k4) * 128
                r = it % 2
                wt = W(r)[:, :cw]
                ab = W(2 + r)[:, :cw]
                pb = ps[f"pb{6 + r}"][:, :cw]
                wqh = B(2 + r)[:, :cw]
                wql = B(4 + r)[:, :cw]
                if it >= 2:
                    c.wait("sync", dv1, dv_done[it - 2])
                v_ld = c.dma("sync", wt,
                             dram[srcn][mo, :, 128 * k4:128 * k4 + cw], s_ld[r])
                c.wait("scalar", s_ld[r], v_ld)
                if first:
                    c.wait("scalar", s_res, v_res)
                    c.wait("tensor", s_res, v_res)
                    first = False
                if it >= 2:
                    c.wait("scalar", pe1, pe_done[it - 2])
                    c.wait("scalar", dv1, dv_done[it - 2])
                c.act(ab, wt, AF.Abs, sem=ac1)
                v_abs = ac1.n
                c.wait("tensor", ac1, v_abs)
                if it >= 2:
                    c.wait("tensor", dv1, dv_done[it - 2])
                pe_done[it] = c.mm(pb, sb["ones"][:], ab, True, True, sem=pe1)
                c.wait("vector", pe1, pe_done[it])
                if it >= 2:
                    c.wait("vector", s_st[r], st_done[it - 2])
                sc = W(4)[:, :cw]
                hs = W(5)[:, :cw]
                hsn = W(6)[:, :cw]
                q1 = W(7)[:, :cw]
                q2 = ab
                tern, shi, slo = B(0)[:, :cw], B(1)[:, :cw], B(6)[:, :cw]
                c.ts2(sc, pb, 1.0 / 128.0, EPS, OP.mult, OP.add, sem=False)
                c.tss(hs, sc, 0.5, OP.mult, sem=False)
                c.tss(hsn, sc, -0.5, OP.mult, sem=False)
                c.tt(q1, wt, hs, OP.is_ge, sem=False)
                c.tt(q2, wt, hsn, OP.is_le, sem=False)
                c.tt(tern, q1, q2, OP.subtract, sem=False)
                c.copy(shi, sc, sem=False)
                c.tt(slo, sc, shi, OP.subtract, sem=False)
                c.tt(wqh, tern, shi, OP.mult, sem=False)
                dv_done[it] = c.tt(wql, tern, slo, OP.mult, sem=dv1)
                c.wait("sync", dv1, dv_done[it])
                c.dma("sync", dram[dsth][mo, :, 128 * k4:128 * k4 + cw], wqh,
                      s_st[r])
                c.dma("sync", dram[dstl][mo, :, 128 * k4:128 * k4 + cw], wql,
                      s_st[r])
                st_done[it] = s_st[r].n
                it += 1
                yield


def emit_s2(c):
    sb, ps, dram, ar = c.sb, c.ps, c.dram, c.ar
    ar.reset()
    xch = ar.f32(16 * TC)
    xnb = ar.f32(16 * TC)
    sbb = ar.f32(TC)
    sqb = ar.f32(16 * TC)

    s_x = [c.s("s2xA"), c.s("s2xB")]
    s_st = c.s("s2st")
    s_g = c.s("s2g")
    for tc in range(NTC):
        col = tc * TC
        c.fence_loads()
        sx = s_x[tc % 2]
        for i in range(16):
            c.dma("sync", xch[:, TC * i:TC * (i + 1)],
                  dram["x_f"][128 * i:128 * (i + 1), col:col + TC], sx)
        v_ld = sx.n

        pe_sq = {}
        ln_block(c, xch, 16, TC, "pb0", "pb1", sqb, sx, v_ld, pe_sq)
        m_b, var = sb["wb0"][:, :TC], sb["wb1"][:, :TC]
        r_b, tmp = sb["wb2"][:, :TC], sb["wb3"][:, :TC]
        ln_finish(c, "pb0", "pb1", TC, m_b, var, r_b, tmp)
        for i in range(16):
            c.run_filler(2)
            xi = xch[:, TC * i:TC * (i + 1)]
            xn = xnb[:, TC * i:TC * (i + 1)]
            c.tt(xn, xi, m_b, OP.subtract, sem=False)
            c.tt(xn, xn, r_b, OP.mult, sem=(i == 15))
        v_xn = c.dv.n

        pe_sq = {}
        ln_block(c, xnb, 16, TC, "pb2", "pb3", sqb, c.dv, v_xn, pe_sq)
        m1, var1 = sb["wb4"][:, :TC], sb["wb5"][:, :TC]
        r1 = sb["wb6"][:, :TC]
        ln_finish(c, "pb2", "pb3", TC, m1, var1, r1, tmp)
        t1 = sb["wb7"][:, :TC]
        u1f = sb["wb8"][:, :TC]
        st_i = {}
        for i in range(16):
            c.run_filler(2)
            rr = i % 2
            xn = xnb[:, TC * i:TC * (i + 1)]
            c.tt(t1, xn, m1, OP.subtract, sem=False)
            v_dv = c.tt(t1, t1, r1, OP.mult)
            c.wait("scalar", c.dv, v_dv)
            v_dv = c.stt(u1f, t1, sb["ln1w"][:, i:i + 1],
                         sb["ln1b"][:, i:i + 1].broadcast_to([128, TC]),
                         OP.mult, OP.add)
            uh, ul = sb[f"bb{rr}"][:, :TC], sb[f"bb{2 + rr}"][:, :TC]
            xh, xl = sb[f"bb{4 + rr}"][:, :TC], sb[f"bb{6 + rr}"][:, :TC]
            if i >= 2:
                c.wait("vector", s_st, st_i[i - 2])
            c.copy(uh, u1f, sem=False)
            v_dv = c.tt(ul, u1f, uh, OP.subtract)
            c.wait("sync", c.dv, v_dv)
            c.dma("sync", dram["u1h"][128 * i:128 * (i + 1), col:col + TC], uh,
                  s_st)
            c.dma("sync", dram["u1l"][128 * i:128 * (i + 1), col:col + TC], ul,
                  s_st)
            c.copy(xh, xn, sem=False)
            v_dv = c.tt(xl, xn, xh, OP.subtract)
            c.wait("sync", c.dv, v_dv)
            c.dma("sync", dram["xnh"][128 * i:128 * (i + 1), col:col + TC], xh,
                  s_st)
            v_st = c.dma("sync",
                         dram["xnl"][128 * i:128 * (i + 1), col:col + TC], xl,
                         s_st)
            st_i[i] = v_st

        acc = sb["wb9"][:, :TC]
        for i in range(16):
            c.run_filler(2)
            ab = sqb[:, TC * (i % 2):TC * (i % 2) + TC]
            if i >= 2:
                c.wait("scalar", c.dv, dv_ab[i - 2])
            v_ab = c.act(ab, xnb[:, TC * i:TC * (i + 1)], AF.Abs)
            c.wait("vector", c.ac, v_ab)
            if i == 0:
                dv_ab = {}
                dv_ab[i] = c.copy(acc, ab)
            else:
                dv_ab[i] = c.tt(acc, acc, ab, OP.max)
        v_acc = c.dv.n
        c.wait("tensor", c.dv, v_acc)
        atok = sb["wb10"][:, :TC // 128]
        for cb in range(TC // 128):
            v_tp = c.transpose(ps["pb4"][:, :128], acc[:, 128 * cb:128 * (cb + 1)])
            c.wait("vector", c.pe, v_tp)
            v_rm = c.redmax(atok[:, cb:cb + 1], ps["pb4"][:, :128])
            c.wait("tensor", c.dv, v_rm)
        c.wait("vector", c.dv, c.dv.n)
        stok = sb["wb11"][:, :TC // 128]
        rtok = sb["wb9"][:, :TC // 128]
        c.tss(atok, atok, EPS, OP.max, sem=False)
        c.recip(stok, atok, sem=False)
        c.tss(stok, stok, 127.0, OP.mult, sem=False)
        v_dv = c.recip(rtok, stok)
        c.wait("sync", c.dv, v_dv)
        sf = dram["sflat"][0, col:col + TC].rearrange("(cc p) -> p cc", p=128)
        rf = dram["rflat"][0, col:col + TC].rearrange("(cc p) -> p cc", p=128)
        c.dma("sync", sf, stok, s_st, slow=True)
        v_st = c.dma("sync", rf, rtok, s_st, slow=True)
        c.wait("gpsimd", s_st, v_st)
        for sl in range(8):
            v_gd = c.dma("gpsimd", _rows(sbb, 16 * sl, 16),
                         dram["sflat"][0:1, col:col + TC].broadcast_to([16, TC]),
                         s_g)
        c.wait("vector", s_g, v_gd)

        v = sb["wb7"][:, :TC]
        t0 = sb["wb8"][:, :TC]
        tb = sb["wb10"][:, :TC]
        db = sb["wb11"][:, :TC]
        st_q = {}
        for i in range(16):
            c.run_filler(2)
            xn = xnb[:, TC * i:TC * (i + 1)]
            c.tt(v, xn, sbb, OP.mult, sem=False)
            qt = sb[f"bb{4 + i % 2}"][:, :TC]
            if i >= 2:
                c.wait("vector", s_st, st_q[i - 2])
            v_dv = ladder_round_clip(c, v, t0, tb, db, qt)
            c.wait("sync", c.dv, v_dv)
            st_q[i] = c.dma("sync",
                            dram["qb"][128 * i:128 * (i + 1), col:col + TC], qt,
                            s_st)


def emit_s3(c):
    sb, ps, dram, ar = c.sb, c.ps, c.dram, c.ar
    ar.reset()
    u1h = ar.bf16(16 * TC2)
    u1l = ar.bf16(16 * TC2)
    blkw = [ar.f32(2048), ar.f32(2048)]
    blkwh = [ar.bf16(2048), ar.bf16(2048)]
    blkwl = [ar.bf16(2048), ar.bf16(2048)]
    xmb = [ar.f32(TC2 + 3), ar.f32(TC2 + 3)]
    cvb = [ar.f32(TC2), ar.f32(TC2)]
    xct = [ar.f32(TC2), ar.f32(TC2)]

    s_u = [c.s("s3uA"), c.s("s3uB")]
    s_w = [c.s("s3wA"), c.s("s3wB")]
    s_g = [c.s("s3gA"), c.s("s3gB")]
    s_st = c.s("s3st")

    def split(mo, sub_lo, pe_guard):
        r = mo % 2
        if sub_lo[mo - 2] is not None:
            c.wait("sync", c.dv, sub_lo[mo - 2])
        v_w = c.dma("sync", blkw[r], dram["wt_inproj"][mo, :, :], s_w[r])
        c.wait("gpsimd", s_w[r], v_w)
        if pe_guard is not None:
            c.wait("gpsimd", c.pe, pe_guard)
        v_hi = c.dma("gpsimd", blkwh[r], blkw[r], s_g[r])
        c.wait("vector", s_g[r], v_hi)
        if pe_guard is not None:
            c.wait("vector", c.pe, pe_guard)
        sub_lo[mo] = c.tt(blkwl[r], blkw[r], blkwh[r], OP.subtract)

    for tc2 in range(NT2):
        col = tc2 * TC2
        c.fence_loads()
        su = s_u[tc2 % 2]
        for i in range(16):
            c.dma("sync", u1h[:, TC2 * i:TC2 * (i + 1)],
                  dram["u1h"][128 * i:128 * (i + 1), col:col + TC2], su)
            c.dma("sync", u1l[:, TC2 * i:TC2 * (i + 1)],
                  dram["u1l"][128 * i:128 * (i + 1), col:col + TC2], su)
        v_u1 = su.n
        c.wait("tensor", su, v_u1)

        import collections
        sub_lo = collections.defaultdict(lambda: None)
        pe_blk, st_done, conv_done, silu_ac, evac_done = {}, {}, {}, {}, {}
        split(0, sub_lo, None)
        for mo in range(64):
            c.run_filler(7)
            r = mo % 2
            if mo + 1 < 64:
                split(mo + 1, sub_lo, pe_blk.get(mo - 1))
            c.wait("tensor", c.dv, sub_lo[mo])
            if mo >= 2:
                c.wait("tensor", evac_done[mo - 2])
            bs = 2 * (mo % 2)
            v_mm = None
            for n0, nw in spans(TC2):
                pbk = ps[f"pb{bs + (n0 // 512) % 2}"]
                for k in range(16):
                    wh = blkwh[r][:, 128 * k:128 * (k + 1)]
                    wl = blkwl[r][:, 128 * k:128 * (k + 1)]
                    uh = u1h[:, TC2 * k + n0:TC2 * k + n0 + nw]
                    ul = u1l[:, TC2 * k + n0:TC2 * k + n0 + nw]
                    c.mm(pbk[:, :nw], wh, uh, start=(k == 0), stop=False,
                         sem=False)
                    c.mm(pbk[:, :nw], wh, ul, start=False, stop=False, sem=False)
                    v_mm = c.mm(pbk[:, :nw], wl, uh, start=False, stop=(k == 15))
            pe_blk[mo] = c.pe.n
            if mo < 32:
                c.wait("scalar", c.pe, v_mm)
                if mo >= 2:
                    c.wait("scalar", c.dv, conv_done[mo - 2])
                v_ev = None
                for n0, nw in spans(TC2):
                    v_ev = c.act(xmb[r][:, 3 + n0:3 + n0 + nw],
                                 ps[f"pb{bs + (n0 // 512) % 2}"][:, :nw], AF.Copy)
                evac_done[mo] = (c.ac, v_ev)
                c.wait("vector", c.ac, v_ev)
                if mo >= 2:
                    c.wait("vector", c.ac, silu_ac[mo - 2])
                if tc2 == 0:
                    c.memset(xmb[r][:, 0:3], 0.0, sem=False)
                else:
                    c.copy(xmb[r][:, 0:3], sb["halo"][:, 3 * mo:3 * mo + 3],
                           sem=False)
                c.ts2(cvb[r], xmb[r][:, 0:TC2], sb["convw"][:, 4 * mo:4 * mo + 1],
                      None, OP.mult, OP.bypass, sem=False)
                for kk in range(1, 4):
                    c.stt(cvb[r], xmb[r][:, kk:kk + TC2],
                          sb["convw"][:, 4 * mo + kk:4 * mo + kk + 1], cvb[r],
                          OP.mult, OP.add, sem=False)
                conv_done[mo] = c.copy(sb["halo"][:, 3 * mo:3 * mo + 3],
                                       xmb[r][:, TC2:TC2 + 3])
                c.wait("scalar", c.dv, conv_done[mo])
                if mo >= 2:
                    c.wait("scalar", s_st, st_done[mo - 2])
                silu_ac[mo] = c.act(xct[r], cvb[r], AF.Silu,
                                    bias=sb["convb"][:, mo:mo + 1])
                c.wait("sync", c.ac, silu_ac[mo])
                st_done[mo] = c.dma(
                    "sync", dram["xc"][128 * mo:128 * (mo + 1), col:col + TC2],
                    xct[r], s_st)
            else:
                zi = mo - 32
                c.wait("scalar", c.pe, v_mm)
                if mo >= 2:
                    c.wait("scalar", s_st, st_done[mo - 2])
                v_ac = None
                for n0, nw in spans(TC2):
                    v_ac = c.act(xct[r][:, n0:n0 + nw],
                                 ps[f"pb{bs + (n0 // 512) % 2}"][:, :nw], AF.Silu)
                silu_ac[mo] = v_ac
                conv_done[mo] = c.dv.n
                evac_done[mo] = (c.ac, v_ac)
                c.wait("sync", c.ac, v_ac)
                st_done[mo] = c.dma(
                    "sync", dram["sz"][128 * zi:128 * (zi + 1), col:col + TC2],
                    xct[r], s_st)


def emit_s3b(c):
    sb, ps, dram, ar = c.sb, c.ps, c.dram, c.ar
    ar.reset()
    xpjh = ar.bf16(32 * 160)
    xpjl = ar.bf16(32 * 160)
    dtpjh = ar.bf16(4096)
    dtpjl = ar.bf16(4096)
    m0 = ar.mark()
    xpjf = ar.f32(32 * 160)
    dtpjf = ar.f32(4096)
    s_pj = c.s("s3bpj")
    c.dma("sync", xpjf, dram["wt_xproj"][:], s_pj)
    v_pj = c.dma("sync", dtpjf, dram["wt_dtproj"][:], s_pj)
    c.wait("gpsimd", s_pj, v_pj)
    pair_split(c, xpjf, xpjh, xpjl, c.s("s3bpg"))
    v_pj2 = pair_split(c, dtpjf, dtpjh, dtpjl, c.s("s3bpg2"))
    c.wait("tensor", c.dv, v_pj2)
    ar.rewind(m0)
    xcch = ar.bf16(32 * TC)
    xccl = ar.bf16(32 * TC)
    xcb4 = [ar.f32(4 * TC), ar.f32(4 * TC)]
    dtlf = ar.f32(TC)
    dtlh = ar.bf16(TC)
    dtll = ar.bf16(TC)
    dtt = sb["wb5"][:, :TC]
    ut = sb["wb6"][:, :TC]

    for tc in range(NTC):
        col = tc * TC
        c.fence_loads()
        dv_sp = {}
        s_x = [c.s("s3bxA"), c.s("s3bxB")]
        s_g = [c.s("s3bgA"), c.s("s3bgB")]
        s_st = c.s("s3bst")
        for jb in range(8):
            c.run_filler(2)
            r = jb % 2
            if jb >= 2:
                c.wait("sync", c.dv, dv_sp[jb - 2])
            v_ld = None
            for u in range(4):
                j = 4 * jb + u
                v_ld = c.dma("sync", xcb4[r][:, TC * u:TC * (u + 1)],
                             dram["xc"][128 * j:128 * (j + 1), col:col + TC],
                             s_x[r])
            c.wait("gpsimd", s_x[r], v_ld)
            v_hi = c.dma("gpsimd", xcch[:, 4 * TC * jb:4 * TC * (jb + 1)],
                         xcb4[r], s_g[r])
            c.wait("vector", s_g[r], v_hi)
            dv_sp[jb] = c.tt(xccl[:, 4 * TC * jb:4 * TC * (jb + 1)], xcb4[r],
                             xcch[:, 4 * TC * jb:4 * TC * (jb + 1)], OP.subtract)
        c.wait("tensor", c.dv, c.dv.n)

        v_mm = None
        for k in range(32):
            st0, sp0 = (k == 0), (k == 31)
            ph = xpjh[:, 160 * k:160 * k + 128]
            pl = xpjl[:, 160 * k:160 * k + 128]
            ph2 = xpjh[:, 160 * k + 128:160 * (k + 1)]
            pl2 = xpjl[:, 160 * k + 128:160 * (k + 1)]
            for n0, nw in spans(TC):
                xh = xcch[:, TC * k + n0:TC * k + n0 + nw]
                xl = xccl[:, TC * k + n0:TC * k + n0 + nw]
                c.mm(ps["pb0"][:, :nw], ph, xh, start=st0, stop=False, sem=False)
                c.mm(ps["pb0"][:, :nw], ph, xl, start=False, stop=False, sem=False)
                c.mm(ps["pb0"][:, :nw], pl, xh, start=False, stop=sp0, sem=sp0)
                c.mm(ps["pb1"][:32, :nw], ph2, xh, start=st0, stop=False, sem=False)
                c.mm(ps["pb1"][:32, :nw], ph2, xl, start=False, stop=False,
                     sem=False)
                v_mm = c.mm(ps["pb1"][:32, :nw], pl2, xh, start=False, stop=sp0,
                            sem=sp0)
        c.wait("vector", c.pe, c.pe.n)
        c.copy(dtlf, ps["pb0"][:, :TC], sem=False)
        c.copy(sb["wb0"][:32, :TC], ps["pb1"][:32, :TC], sem=False)
        c.copy(dtlh, dtlf, sem=False)
        v_dv = c.tt(dtll, dtlf, dtlh, OP.subtract)
        c.wait("sync", c.dv, v_dv)
        c.dma("sync", dram["bcflat"][:, col:col + TC], sb["wb0"][:32, :TC], s_st)

        c.wait("tensor", c.dv, v_dv)
        for mo in range(32):
            c.run_filler(2)
            r = mo % 2
            wh = dtpjh[:, 128 * mo:128 * (mo + 1)]
            wl = dtpjl[:, 128 * mo:128 * (mo + 1)]
            pbk = ps[f"pb{2 + mo % 2}"]
            if mo >= 2:
                c.wait("sync", c.dv, dv_sp[mo - 2 + 100])
            v_xc = c.dma("sync", xcb4[r][:, :TC],
                         dram["xc"][128 * mo:128 * (mo + 1), col:col + TC],
                         s_x[r])
            v_mm = None
            for n0, nw in spans(TC):
                c.mm(pbk[:, :nw], wh, dtlh[:, n0:n0 + nw], start=True, stop=False,
                     sem=False)
                c.mm(pbk[:, :nw], wh, dtll[:, n0:n0 + nw], start=False, stop=False,
                     sem=False)
                v_mm = c.mm(pbk[:, :nw], wl, dtlh[:, n0:n0 + nw], start=False,
                            stop=True)
            c.wait("vector", c.pe, v_mm)
            c.ts2(dtt, pbk[:, :TC], sb["dtb"][:, mo:mo + 1], 80.0, OP.add, OP.min,
                  sem=False)
            v_dv = c.copy(dtt, dtt)
            c.wait("scalar", c.dv, v_dv)
            c.act(dtt, dtt, AF.Exp, sem=False)
            v_ac = c.act(dtt, dtt, AF.Ln, bias=1.0)
            c.wait("vector", c.ac, v_ac)
            c.wait("vector", s_x[r], v_xc)
            v_dv = c.tt(ut, dtt, xcb4[r][:, :TC], OP.mult)
            dv_sp[mo + 100] = v_dv
            c.wait("sync", c.ac, v_ac)
            c.wait("sync", c.dv, v_dv)
            c.dma("sync", dram["dt"][128 * mo:128 * (mo + 1), col:col + TC], dtt,
                  s_st)
            v_st = c.dma("sync",
                         dram["uu"][128 * mo:128 * (mo + 1), col:col + TC], ut,
                         s_st)
            c.wait("vector", s_st, v_st)
            c.wait("scalar", s_st, v_st)
            c.wait("tensor", c.dv, v_dv)


def emit_s4(c):
    sb, ps, dram, ar = c.sb, c.ps, c.dram, c.ar
    ar.reset()
    rsb = ar.f32(T)
    wqh = [ar.bf16(2048), ar.bf16(2048)]
    wql = [ar.bf16(2048), ar.bf16(2048)]
    sgb = ar.f32(T)
    hidt = [c.sb["wb1"][:, :512], c.sb["wb4"][:, :512]]
    qmark = ar.mark()
    qbuf = ar.bf16(16 * T)

    s_q = c.s("s4q")
    s_g = c.s("s4g")
    s_w = [c.s("s4wA"), c.s("s4wB")]
    s_st = c.s("s4st")
    s_h = [c.s("s4hA"), c.s("s4hB")]
    for i in range(16):
        c.dma("sync", qbuf[:, T * i:T * (i + 1)],
              dram["qb"][128 * i:128 * (i + 1), :], s_q)
    v_gd = None
    for sl in range(8):
        v_gd = c.dma("gpsimd", _rows(rsb, 16 * sl, 16),
                     dram["rflat"][0:1, :].broadcast_to([16, T]), s_g)
    v_q = s_q.n
    c.wait("tensor", s_q, v_q)
    c.wait("vector", s_g, v_gd)

    pe_done, ev_gate, ev_up, st_hid = {}, {}, {}, {}
    for mo in range(43):
        for br, (srch, srcl, pboff) in enumerate(
                [("wqgh", "wqgl", 0), ("wquh", "wqul", 4)]):
            r = br
            if mo >= 1:
                c.wait("sync", c.pe, pe_done[(mo - 1, br)])
            c.dma("sync", wqh[r], dram[srch][mo, :, :], s_w[r])
            v_w = c.dma("sync", wql[r], dram[srcl][mo, :, :], s_w[r])
            c.wait("tensor", s_w[r], v_w)
            for n0, nw in spans(T):
                ni = n0 // 512
                pbk = ps[f"pb{pboff + ni}"]
                if mo >= 1:
                    c.wait("tensor", (ev_gate if br == 0 else ev_up)
                           .get((mo - 1, ni), (c.dv, 0)))
                v_mm = None
                for k in range(16):
                    qk = qbuf[:, T * k + n0:T * k + n0 + nw]
                    c.mm(pbk[:, :nw], wqh[r][:, 128 * k:128 * (k + 1)], qk,
                         start=(k == 0), stop=False, sem=False)
                    v_mm = c.mm(pbk[:, :nw], wql[r][:, 128 * k:128 * (k + 1)], qk,
                                start=False, stop=(k == 15))
                c.wait("vector", c.pe, v_mm)
                if br == 0:
                    v_dv = c.tt(sb["wb0"][:, :nw], pbk[:, :nw],
                                rsb[:, n0:n0 + nw], OP.mult)
                    c.wait("scalar", c.dv, v_dv)
                    v_ac = c.act(sgb[:, n0:n0 + nw], sb["wb0"][:, :nw], AF.Silu)
                    ev_gate[(mo, ni)] = (c.ac, v_ac)
                else:
                    c.wait("vector", ev_gate[(mo, ni)])
                    hidx = mo * (T // 512) + ni
                    hb = hidt[hidx % 2]
                    if hidx >= 2:
                        c.wait("vector", s_st, st_hid[hidx - 2])
                    c.tt(sb["wb0"][:, :nw], pbk[:, :nw], rsb[:, n0:n0 + nw],
                         OP.mult, sem=False)
                    v_dv = c.tt(hb[:, :nw], sb["wb0"][:, :nw], sgb[:, n0:n0 + nw],
                                OP.mult)
                    ev_up[(mo, ni)] = (c.dv, v_dv)
                    c.wait("sync", c.dv, v_dv)
                    v_st = c.dma("sync",
                                 dram["hid"][128 * mo:128 * (mo + 1), n0:n0 + nw],
                                 hb[:, :nw], s_st)
                    st_hid[hidx] = v_st
            pe_done[(mo, br)] = c.pe.n

    c.fence_loads()
    ar.rewind(qmark)
    accb = ar.f32(T)
    hjb = [ar.f32(T), ar.f32(T)]
    habs = [ar.f32(T), ar.f32(T)]
    dv_acc = {}
    ac_abs = {}
    for j in range(43):
        r = j % 2
        if j >= 2:
            c.wait("sync", c.ac, ac_abs[j - 2])
        v_h = c.dma("sync", hjb[r], dram["hid"][128 * j:128 * (j + 1), :], s_h[r])
        c.wait("scalar", s_h[r], v_h)
        if j >= 2:
            c.wait("scalar", c.dv, dv_acc[j - 2])
        ac_abs[j] = c.act(habs[r], hjb[r], AF.Abs)
        c.wait("vector", c.ac, ac_abs[j])
        if j == 0:
            dv_acc[j] = c.copy(accb, habs[r])
        else:
            dv_acc[j] = c.tt(accb, accb, habs[r], OP.max)
    v_acc = c.dv.n
    c.wait("tensor", c.dv, v_acc)
    atok = sb["wb10"][:, :T // 128]
    for cb in range(T // 128):
        v_tp = c.transpose(ps["pb4"][:, :128], accb[:, 128 * cb:128 * (cb + 1)])
        c.wait("vector", c.pe, v_tp)
        v_rm = c.redmax(atok[:, cb:cb + 1], ps["pb4"][:, :128])
        c.wait("tensor", c.dv, v_rm)
    c.wait("vector", c.dv, c.dv.n)
    stok = sb["wb11"][:, :T // 128]
    rtok = sb["wb9"][:, :T // 128]
    c.tss(atok, atok, EPS, OP.max, sem=False)
    c.recip(stok, atok, sem=False)
    c.tss(stok, stok, 127.0, OP.mult, sem=False)
    v_dv = c.recip(rtok, stok)
    c.wait("sync", c.dv, v_dv)
    s2f = dram["s2flat"][0, :].rearrange("(cc p) -> p cc", p=128)
    r2f = dram["r2flat"][0, :].rearrange("(cc p) -> p cc", p=128)
    c.dma("sync", s2f, stok, s_st, slow=True)
    v_st = c.dma("sync", r2f, rtok, s_st, slow=True)
    c.wait("gpsimd", s_st, v_st)
    s_g2 = c.s("s4g2")
    for sl in range(8):
        c.dma("gpsimd", _rows(sb["s2b"][:], 16 * sl, 16),
              dram["s2flat"][0:1, :].broadcast_to([16, T]), s_g2)
        v_gd = c.dma("gpsimd", _rows(sb["r2b"][:], 16 * sl, 16),
                     dram["r2flat"][0:1, :].broadcast_to([16, T]), s_g2)
    c.wait("vector", s_g2, v_gd)

    vv = ar.f32(T)
    t0 = ar.f32(T)
    tb = ar.f32(T)
    db = ar.f32(T)
    hqt = ar.bf16(T)
    for j in range(43):
        r = j % 2
        if j >= 2:
            c.wait("sync", c.dv, dv_acc[j - 2])
        v_h = c.dma("sync", hjb[r], dram["hid"][128 * j:128 * (j + 1), :], s_h[r])
        c.wait("vector", s_h[r], v_h)
        c.tt(vv, hjb[r], sb["s2b"][:], OP.mult, sem=False)
        dv_acc[j] = ladder_round_clip(c, vv, t0, tb, db, hqt)
        c.wait("sync", c.dv, dv_acc[j])
        v_st = c.dma("sync", dram["hqb"][128 * j:128 * (j + 1), :], hqt, s_st)
        c.wait("vector", s_st, v_st)


def emit_s5(c):
    sb, ps, dram, ar = c.sb, c.ps, c.dram, c.ar
    wdh = ar.bf16(5504)
    wdl = ar.bf16(5504)
    hqt = [ar.bf16(512) for _ in range(4)]
    dot = sb["wb0"][:, :512]
    s_w = c.s("s5w")
    s_hh = [c.s(f"s5h{i}") for i in range(4)]
    s_st = c.s("s5st")
    pe_prev = None
    for mo in range(16):
        if pe_prev is not None:
            c.wait("sync", c.pe, pe_prev)
        c.dma("sync", wdh, dram["wqdh"][mo, :, :], s_w)
        v_w = c.dma("sync", wdl, dram["wqdl"][mo, :, :], s_w)
        c.wait("tensor", s_w, v_w)
        for tc in range(NTC):
            col = tc * TC
            pbk = ps[f"pb{6 + (tc % 2)}"]
            v_mm = {}
            v_h = {}
            for k in range(43):
                rr = k % 4
                if k >= 4:
                    c.wait("sync", c.pe, v_mm[k - 4])
                v_h[k] = c.dma("sync", hqt[rr][:, :TC],
                               dram["hqb"][128 * k:128 * (k + 1), col:col + TC],
                               s_hh[rr])
                c.wait("tensor", s_hh[rr], v_h[k])
                c.mm(pbk[:, :TC], wdh[:, 128 * k:128 * (k + 1)], hqt[rr][:, :TC],
                     start=(k == 0), stop=False, sem=False)
                v_mm[k] = c.mm(pbk[:, :TC], wdl[:, 128 * k:128 * (k + 1)],
                               hqt[rr][:, :TC], start=False, stop=(k == 42))
            c.wait("vector", c.pe, v_mm[42])
            v_dv = c.tt(dot[:, :TC], pbk[:, :TC],
                        sb["r2b"][:, col:col + TC], OP.mult)
            c.wait("sync", c.dv, v_dv)
            v_st = c.dma("sync", dram["dwn"][128 * mo:128 * (mo + 1),
                                             col:col + TC], dot[:, :TC], s_st)
            c.wait("vector", s_st, v_st)
            c.wait("tensor", c.dv, v_dv)
            pe_prev = c.pe.n
            yield


def emit_s6(c):
    sb, ps, dram, ar = c.sb, c.ps, c.dram, c.ar
    ar.reset()
    Bb = ar.f32(T)
    Cb = ar.f32(T)
    dtb = [ar.f32(T), ar.f32(T)]
    dab = dtb
    bsc = ar.f32(T)
    hsc = [ar.f32(T), ar.f32(T)]
    hcb = [ar.f32(T), ar.f32(T)]
    xcj = ar.f32(T)
    szj = ar.f32(T)
    ygf = c.sb["wb2"][:, :512]
    ygw = c.sb["wb3"][:, :512]
    pselw = [sb[f"wb{4 + i}"] for i in range(4)]
    prepw = [sb[f"wb{8 + i}"] for i in range(4)]

    def psel_ap(kk):
        return pselw[kk // 4][:, 128 * (kk % 4):128 * (kk % 4) + 128]

    def prep_ap(kk):
        return prepw[kk // 4][:, 128 * (kk % 4):128 * (kk % 4) + 128]

    dtsrc = ar.f32(T)
    usj = ar.f32(T)
    s_x = c.s("s6x")
    s_bc = c.s("s6bc")
    s_d = [c.s("s6dA"), c.s("s6dB")]
    s_u = [c.s("s6uA"), c.s("s6uB")]
    s_st = c.s("s6st")
    s_ps = c.s("s6ps")
    for i in range(4):
        c.dma("sync", pselw[i][:, :512],
              c.dram["psel_sb"][:, 512 * i:512 * (i + 1)], s_ps)
        v_ps = c.dma("sync", prepw[i][:, :512],
                     c.dram["prep_sb"][:, 512 * i:512 * (i + 1)], s_ps)
    c.wait("tensor", s_ps, v_ps)

    v_gd = None
    for sl in range(8):
        c.dma("gpsimd", _rows(Bb, 16 * sl, 16), dram["bcflat"][0:16, :], s_bc)
        v_gd = c.dma("gpsimd", _rows(Cb, 16 * sl, 16), dram["bcflat"][16:32, :],
                     s_bc)
    c.wait("vector", s_bc, v_gd)
    c.wait("scalar", s_bc, v_gd)
    c.wait("gpsimd", s_bc, v_gd)

    dv_done, pe_done, gp_done = {}, {}, {}
    pe_last, v_dsrc = {}, {}

    def load_src(jj):
        v_dsrc[jj] = c.dma("sync", dtsrc,
                           dram["dt"][128 * jj:128 * (jj + 1), :], s_d[jj % 2])

    load_src(0)
    dv_evac = 0
    ac_ev = dv_bev = None
    for j in range(32):
        rj = j % 2
        c.wait("sync", c.dv, dv_evac)
        v_x = c.dma("sync", xcj, dram["xc"][128 * j:128 * (j + 1), :], s_x)
        v_s = c.dma("sync", szj, dram["sz"][128 * j:128 * (j + 1), :], s_x)
        c.wait("vector", s_d[j % 2], v_dsrc[j])
        c.wait("vector", s_x, v_s)
        if j >= 1:
            c.wait("vector", c.pe, pe_last[j - 1])
        v_us = c.tt(usj, dtsrc, xcj, OP.mult)
        c.wait("tensor", s_d[j % 2], v_dsrc[j])
        c.wait("tensor", c.dv, v_us)
        for kk in range(16):
            c.run_filler(1)
            k = 16 * j + kk
            r = k % 2
            for ci, (n0, nw) in enumerate(spans(T)):
                if ac_ev is not None:
                    c.wait("tensor", c.ac, ac_ev)
                v_md = c.mm(ps["pb4"][:, :nw], prep_ap(kk),
                            dtsrc[:, n0:n0 + nw], start=True, stop=True,
                            sem=True)
                if dv_bev is not None:
                    c.wait("tensor", c.dv, dv_bev)
                v_mu = c.mm(ps["pb5"][:, :nw], prep_ap(kk),
                            usj[:, n0:n0 + nw], start=True, stop=True,
                            sem=True)
                c.wait("scalar", c.pe, v_md)
                if ci == 0 and k >= 2:
                    c.wait("scalar", c.dv, dv_done[k - 2])
                ac_ev = c.act(dab[r][:, n0:n0 + nw], ps["pb4"][:, :nw], AF.Exp,
                              scale=sb["aneg"][:, k:k + 1])
                c.wait("vector", c.pe, v_mu)
                dv_bev = c.tt(bsc[:, n0:n0 + nw], ps["pb5"][:, :nw],
                              Bb[:, n0:n0 + nw], OP.mult)
            c.wait("vector", c.ac, ac_ev)
            if k >= 2:
                c.wait("vector", c.gp, gp_done[k - 2])
            dv_done[k] = c.scan(hsc[r], dab[r], bsc, 0.0)
            c.wait("gpsimd", c.dv, dv_done[k])
            if k >= 2:
                c.wait("gpsimd", c.pe, pe_done[k - 2])
            gp_done[k] = c.tt(hcb[r], hsc[r], Cb, OP.mult, eng="gpsimd")
            c.wait("tensor", c.gp, gp_done[k])
            if kk == 0:
                c.wait("tensor", c.dv, dv_evac)
            nsp = list(spans(T))
            for si, (n0, nw) in enumerate(nsp):
                pe_done[k] = c.mm(ps[f"pb{(n0 // 512) % 4}"][:, :nw],
                                  psel_ap(kk),
                                  hcb[r][:, n0:n0 + nw],
                                  start=(kk == 0), stop=(kk == 15),
                                  sem=(si == len(nsp) - 1))
        pe_last[j] = pe_done[16 * j + 15]
        if j + 1 < 32:
            c.wait("sync", c.pe, pe_last[j])
            load_src(j + 1)
        c.wait("vector", c.pe, pe_done[16 * j + 15])
        c.wait("vector", s_x, v_s)
        for n0, nw in spans(T):
            c.stt(ygf[:, :nw], xcj[:, n0:n0 + nw], sb["dsk"][:, j:j + 1],
                  ps[f"pb{(n0 // 512) % 4}"][:, :nw], OP.mult, OP.add, sem=False)
            c.tt(ygw[:, :nw], ygf[:, :nw], szj[:, n0:n0 + nw], OP.mult, sem=False)
            yh = sb["bb0"][:, :nw]
            yl = sb["bb1"][:, :nw]
            c.copy(yh, ygw[:, :nw], sem=False)
            v_dv = c.tt(yl, ygw[:, :nw], yh, OP.subtract)
            c.wait("sync", c.dv, v_dv)
            c.dma("sync", dram["ygh"][128 * j:128 * (j + 1), n0:n0 + nw], yh,
                  s_st)
            v_st = c.dma("sync", dram["ygl"][128 * j:128 * (j + 1), n0:n0 + nw],
                         yl, s_st)
            c.wait("vector", s_st, v_st)
        dv_evac = c.dv.n


def emit_s7(c):
    sb, ps, dram, ar = c.sb, c.ps, c.dram, c.ar
    ar.reset()
    mark0 = ar.mark()
    yph = ar.bf16(32 * TC)
    ypl = ar.bf16(32 * TC)
    axb = ar.f32(16 * TC)
    opbh = [ar.bf16(4096), ar.bf16(4096)]
    opbl = [ar.bf16(4096), ar.bf16(4096)]
    mark1 = ar.mark()
    ar.rewind(mark0)
    sqb2 = ar.f32(16 * TC)
    ar.rewind(mark1)

    s_y = [c.s("s7yA"), c.s("s7yB")]
    s_w = c.s("s7w")
    s_xn = c.s("s7xn")
    s_st = c.s("s7st")
    sqb = sqb2
    for tc in range(NTC):
        col = tc * TC
        c.fence_loads()
        sy = s_y[tc % 2]
        for j in range(32):
            c.dma("sync", yph[:, TC * j:TC * (j + 1)],
                  dram["ygh"][128 * j:128 * (j + 1), col:col + TC], sy)
            c.dma("sync", ypl[:, TC * j:TC * (j + 1)],
                  dram["ygl"][128 * j:128 * (j + 1), col:col + TC], sy)
        v_ld = sy.n
        c.wait("tensor", sy, v_ld)
        pe_done, dv_ax = {}, {}
        s_w2 = [s_w, c.s("s7w2")]
        for mo in range(16):
            r = mo % 2
            if mo >= 2:
                c.wait("sync", c.pe, pe_done[mo - 2])
            c.dma("sync", opbh[r], dram["oph"][mo, :, :], s_w2[r])
            v_w = c.dma("sync", opbl[r], dram["opl"][mo, :, :], s_w2[r])
            if mo >= 1:
                c.wait("sync", c.dv, dv_ax[mo - 1])
            c.dma("sync", sb["bb6"][:, :TC],
                  dram["xnh"][128 * mo:128 * (mo + 1), col:col + TC], s_xn)
            v_xn = c.dma("sync", sb["bb7"][:, :TC],
                         dram["xnl"][128 * mo:128 * (mo + 1), col:col + TC],
                         s_xn)
            c.wait("tensor", s_w2[r], v_w)
            if mo >= 2:
                c.wait("tensor", c.dv, dv_ax[mo - 2])
            for n0, nw in spans(TC):
                pbk = ps[f"pb{mo % 2}"]
                v_mm = None
                for k in range(32):
                    yh = yph[:, TC * k + n0:TC * k + n0 + nw]
                    yl = ypl[:, TC * k + n0:TC * k + n0 + nw]
                    wh = opbh[r][:, 128 * k:128 * (k + 1)]
                    wl = opbl[r][:, 128 * k:128 * (k + 1)]
                    c.mm(pbk[:, :nw], wh, yh, start=(k == 0), stop=False,
                         sem=False)
                    c.mm(pbk[:, :nw], wh, yl, start=False, stop=False, sem=False)
                    v_mm = c.mm(pbk[:, :nw], wl, yh, start=False, stop=(k == 31))
                c.wait("vector", c.pe, v_mm)
                c.wait("vector", s_xn, v_xn)
                axt = axb[:, TC * mo + n0:TC * mo + n0 + nw]
                c.tt(sb["wb0"][:, :nw], sb["bb6"][:, n0:n0 + nw],
                     sb["bb7"][:, n0:n0 + nw], OP.add, sem=False)
                v_dv = c.tt(axt, sb["wb0"][:, :nw], pbk[:, :nw], OP.add)
                dv_ax[mo] = v_dv
                c.wait("sync", c.dv, v_dv)
                c.dma("sync", dram["anx"][128 * mo:128 * (mo + 1),
                                          col + n0:col + n0 + nw], axt, s_st)
            pe_done[mo] = c.pe.n

        pe_sq = {}
        ln_block(c, axb, 16, TC, "pb2", "pb3", sqb, c.dv, c.dv.n, pe_sq)
        m_b, var = sb["wb0"][:, :TC], sb["wb1"][:, :TC]
        r_b, tmp = sb["wb2"][:, :TC], sb["wb3"][:, :TC]
        ln_finish(c, "pb2", "pb3", TC, m_b, var, r_b, tmp)
        t1 = sb["wb7"][:, :TC]
        u2 = sb["wb8"][:, :TC]
        for i in range(16):
            xi = axb[:, TC * i:TC * (i + 1)]
            c.tt(t1, xi, m_b, OP.subtract, sem=False)
            v_dv = c.tt(t1, t1, r_b, OP.mult)
            c.wait("scalar", c.dv, v_dv)
            v_dv = c.stt(u2, t1, sb["ln2w"][:, i:i + 1],
                         sb["ln2b"][:, i:i + 1].broadcast_to([128, TC]),
                         OP.mult, OP.add)
            lh, ll = sb["bb0"][:, :TC], sb["bb1"][:, :TC]
            c.copy(lh, u2, sem=False)
            v_dv = c.tt(ll, u2, lh, OP.subtract)
            c.wait("sync", c.dv, v_dv)
            c.dma("sync", dram["ln2h"][128 * i:128 * (i + 1), col:col + TC], lh,
                  s_st)
            v_st = c.dma("sync",
                         dram["ln2l"][128 * i:128 * (i + 1), col:col + TC], ll,
                         s_st)
            c.wait("vector", s_st, v_st)


def emit_s8(c):
    sb, ps, dram, ar = c.sb, c.ps, c.dram, c.ar
    ar.reset()
    l2h = ar.bf16(16 * TC2)
    l2l = ar.bf16(16 * TC2)
    blkw = [ar.f32(2048), ar.f32(2048)]
    blkwh = [ar.bf16(2048), ar.bf16(2048)]
    blkwl = [ar.bf16(2048), ar.bf16(2048)]
    sgt = ar.f32(512)
    hgt = ar.f32(512)

    s_l2 = [c.s("s8lA"), c.s("s8lB")]
    s_w = [c.s("s8wA"), c.s("s8wB")]
    s_g = [c.s("s8gA"), c.s("s8gB")]
    s_st = c.s("s8st")

    import collections
    for tc2 in range(NT2):
        col = tc2 * TC2
        c.fence_loads()
        sl2 = s_l2[tc2 % 2]
        for i in range(16):
            c.dma("sync", l2h[:, TC2 * i:TC2 * (i + 1)],
                  dram["ln2h"][128 * i:128 * (i + 1), col:col + TC2], sl2)
            c.dma("sync", l2l[:, TC2 * i:TC2 * (i + 1)],
                  dram["ln2l"][128 * i:128 * (i + 1), col:col + TC2], sl2)
        v_l2 = sl2.n
        c.wait("tensor", sl2, v_l2)

        sub_lo = collections.defaultdict(lambda: None)
        pe_step, dv_hg = {}, {}

        def split(step, pe_guard):
            mo2, half = divmod(step, 2)
            blkid = mo2 if half == 0 else 32 + mo2
            r = step % 2
            if sub_lo[step - 2] is not None:
                c.wait("sync", c.dv, sub_lo[step - 2])
            v_w = c.dma("sync", blkw[r], dram["wt_ffproj"][blkid, :, :], s_w[r])
            c.wait("gpsimd", s_w[r], v_w)
            if pe_guard is not None:
                c.wait("gpsimd", c.pe, pe_guard)
            v_hi = c.dma("gpsimd", blkwh[r], blkw[r], s_g[r])
            c.wait("vector", s_g[r], v_hi)
            if pe_guard is not None:
                c.wait("vector", c.pe, pe_guard)
            sub_lo[step] = c.tt(blkwl[r], blkw[r], blkwh[r], OP.subtract)

        split(0, None)
        for step in range(64):
            c.run_filler(4)
            mo, half = divmod(step, 2)
            r = step % 2
            if step + 1 < 64:
                split(step + 1, pe_step.get(step - 1))
            c.wait("tensor", c.dv, sub_lo[step])
            if mo >= 1:
                c.wait("tensor", dv_hg.get((mo - 1, half), (c.dv, 0)))
            pboff = 0 if half == 0 else 2
            for n0, nw in spans(TC2):
                pbk = ps[f"pb{pboff + (n0 // 512) % 2}"]
                v_mm = None
                for k in range(16):
                    wh = blkwh[r][:, 128 * k:128 * (k + 1)]
                    wl = blkwl[r][:, 128 * k:128 * (k + 1)]
                    lh = l2h[:, TC2 * k + n0:TC2 * k + n0 + nw]
                    ll = l2l[:, TC2 * k + n0:TC2 * k + n0 + nw]
                    c.mm(pbk[:, :nw], wh, lh, start=(k == 0), stop=False,
                         sem=False)
                    c.mm(pbk[:, :nw], wh, ll, start=False, stop=False, sem=False)
                    v_mm = c.mm(pbk[:, :nw], wl, lh, start=False, stop=(k == 15))
                pe_n = getattr(c, "_s8pen", {})
                pe_n[(half, n0)] = v_mm
                c._s8pen = pe_n
            pe_step[step] = c.pe.n
            if half == 1:
                pe_n = c._s8pen
                v_dv = None
                for n0, nw in spans(TC2):
                    pa = ps[f"pb{(n0 // 512) % 2}"]
                    pg = ps[f"pb{2 + (n0 // 512) % 2}"]
                    c.wait("scalar", c.pe, pe_n[(1, n0)])
                    v_ac = c.act(sgt[:, :nw], pg[:, :nw], AF.Silu)
                    c.wait("vector", c.ac, v_ac)
                    c.wait("vector", c.pe, pe_n[(0, n0)])
                    c.tt(hgt[:, :nw], pa[:, :nw], sgt[:, :nw], OP.mult, sem=False)
                    hh, hl = sb["bb0"][:, :nw], sb["bb1"][:, :nw]
                    c.copy(hh, hgt[:, :nw], sem=False)
                    v_dv = c.tt(hl, hgt[:, :nw], hh, OP.subtract)
                    c.wait("sync", c.dv, v_dv)
                    c.dma("sync", dram["hgh"][128 * mo:128 * (mo + 1),
                                              col + n0:col + n0 + nw], hh, s_st)
                    v_st = c.dma("sync", dram["hgl"][128 * mo:128 * (mo + 1),
                                                     col + n0:col + n0 + nw], hl,
                                 s_st)
                    c.wait("vector", s_st, v_st)
                dv_hg[(mo, 0)] = (c.dv, v_dv)
                dv_hg[(mo, 1)] = (c.dv, v_dv)


def emit_s9(c):
    sb, ps, dram, ar = c.sb, c.ps, c.dram, c.ar
    ar.reset()
    hgh = ar.bf16(32 * TC)
    hgl = ar.bf16(32 * TC)
    fobh = [ar.bf16(4096), ar.bf16(4096)]
    fobl = [ar.bf16(4096), ar.bf16(4096)]
    xt = [ar.f32(TC), ar.f32(TC)]
    at = [ar.f32(TC), ar.f32(TC)]
    dwt = [ar.f32(TC), ar.f32(TC)]
    ot = [ar.f32(512), ar.f32(512)]

    s_hg = [c.s("s9hA"), c.s("s9hB")]
    s_w = [c.s("s9wA"), c.s("s9wB")]
    s_r = c.s("s9r")
    s_st = c.s("s9st")
    for tc in range(NTC):
        col = tc * TC
        c.fence_loads()
        shg = s_hg[tc % 2]
        for j in range(32):
            c.dma("sync", hgh[:, TC * j:TC * (j + 1)],
                  dram["hgh"][128 * j:128 * (j + 1), col:col + TC], shg)
            c.dma("sync", hgl[:, TC * j:TC * (j + 1)],
                  dram["hgl"][128 * j:128 * (j + 1), col:col + TC], shg)
        v_ld = shg.n
        c.wait("tensor", shg, v_ld)
        pe_done, dv_res, st_prev = {}, {}, {}
        for mo in range(16):
            r = mo % 2
            if mo >= 2:
                c.wait("sync", c.pe, pe_done[mo - 2])
            c.dma("sync", fobh[r], dram["foh"][mo, :, :], s_w[r])
            v_w = c.dma("sync", fobl[r], dram["fol"][mo, :, :], s_w[r])
            rows = slice(128 * mo, 128 * (mo + 1))
            if mo >= 2:
                c.wait("sync", c.dv, dv_res[mo - 2])
            c.dma("sync", xt[r], dram["x_f"][rows, col:col + TC], s_r)
            c.dma("sync", at[r], dram["anx"][rows, col:col + TC], s_r)
            v_res = c.dma("sync", dwt[r], dram["dwn"][rows, col:col + TC], s_r)
            c.wait("tensor", s_w[r], v_w)
            if mo >= 2:
                c.wait("tensor", c.dv, dv_res[mo - 2])
            for n0, nw in spans(TC):
                pbk = ps[f"pb{mo % 2}"]
                v_mm = None
                for k in range(32):
                    hh = hgh[:, TC * k + n0:TC * k + n0 + nw]
                    hl = hgl[:, TC * k + n0:TC * k + n0 + nw]
                    wh = fobh[r][:, 128 * k:128 * (k + 1)]
                    wl = fobl[r][:, 128 * k:128 * (k + 1)]
                    c.mm(pbk[:, :nw], wh, hh, start=(k == 0), stop=False,
                         sem=False)
                    c.mm(pbk[:, :nw], wh, hl, start=False, stop=False, sem=False)
                    v_mm = c.mm(pbk[:, :nw], wl, hh, start=False, stop=(k == 31))
                c.wait("vector", c.pe, v_mm)
                c.wait("vector", s_r, v_res)
                if mo >= 2:
                    c.wait("vector", s_st, st_prev[mo - 2])
                ob = ot[mo % 2]
                c.tt(ob[:, :nw], pbk[:, :nw], at[r][:, n0:n0 + nw], OP.add,
                     sem=False)
                c.tt(ob[:, :nw], ob[:, :nw], xt[r][:, n0:n0 + nw], OP.add,
                     sem=False)
                v_dv = c.tt(ob[:, :nw], ob[:, :nw], dwt[r][:, n0:n0 + nw], OP.add)
                c.wait("sync", c.dv, v_dv)
                c.dma("sync", dram["out_f"][rows, col + n0:col + n0 + nw],
                      ob[:, :nw], s_st)
                st_prev[mo] = s_st.n
            dv_res[mo] = c.dv.n
            pe_done[mo] = c.pe.n


def prep_inputs(inputs):
    g = _host_layouts(inputs)
    x = np.ascontiguousarray(np.asarray(inputs["L0_attn_x"], dtype=np.float32))
    xT = [np.ascontiguousarray(x[b].T[:, :T]) for b in range(NCORES)]
    return g, xT


def kernel(**inputs):
    g, xT = prep_inputs(inputs)
    nc = build_program()
    in_maps = []
    for b in range(NCORES):
        m = dict(g)
        m["x_f"] = xT[b]
        in_maps.append(m)
    res = run_bass_kernel_spmd(nc, in_maps, core_ids=list(range(NCORES)))
    out = np.stack([np.asarray(r["out_f"]).T for r in res.results])
    return out.astype(np.float32)


# version 12 (final)
# speedup vs baseline: 1.1115x; 1.1115x over previous
import os as _os
import numpy as np
import concourse.bass as bass
import concourse.mybir as mybir
from concourse.bass_utils import run_bass_kernel_spmd
from concourse.alu_op_type import AluOpType as OP

F32 = mybir.dt.float32
BF16 = mybir.dt.bfloat16
AF = mybir.ActivationFunctionType
AX = mybir.AxisListType

T = int(_os.environ.get("K_T", "2048"))
D = 2048
DI = 4096
DFF = 5504
TC = min(512, T)
NTC = T // TC
TC2 = min(1024, T)
NT2 = T // TC2
EPS = 1e-5
NCORES = 8

ENGS = ("sync", "tensor", "vector", "scalar", "gpsimd")


def spans(total, step=512):
    return [(i, min(step, total - i)) for i in range(0, total, step)]


class Sem:
    def __init__(self, handle):
        self.h = handle
        self.n = 0


class Arena:

    def __init__(self, nc, base, top):
        self.nc, self.base, self.top = nc, base, top
        self.off = 0
        self.cnt = 0

    def reset(self):
        self.off = 0

    def mark(self):
        return self.off

    def rewind(self, m):
        self.off = m

    def _alloc(self, n, dtype, bsz):
        off = (self.base + self.off + 31) & ~31
        h = self.nc.alloc_sbuf_tensor_at(f"arn{self.cnt}", [128, n], dtype,
                                         offset=off)
        self.cnt += 1
        self.off = off + n * bsz - self.base
        assert self.base + self.off <= self.top, \
            f"arena overflow: {self.base + self.off} > {self.top}"
        return h[:, :]

    def f32(self, n):
        return self._alloc(n, F32, 4)

    def bf16(self, n):
        return self._alloc(n, BF16, 2)


class Ctx:
    def __init__(self, nc, dram, sb, ps, sems, pool_handles, arena):
        self.nc, self.dram, self.sb, self.ps = nc, dram, sb, ps
        self.sems = sems
        self.ar = arena
        self.q = {e: [] for e in ENGS}
        self.pe, self.dv, self.ac, self.gp = (sems["pe"], sems["dv"],
                                              sems["ac"], sems["gp"])
        self._pool_handles = pool_handles
        self._pool = {}

    def s(self, name):
        if name not in self._pool:
            idx = len(self._pool)
            assert idx < len(self._pool_handles), "sem pool exhausted"
            self._pool[name] = Sem(self._pool_handles[idx])
        return self._pool[name]

    def _emit(self, eng, fn, sem, k):
        if sem is None:
            self.q[eng].append(fn)
            return None
        sem.n += k
        h = sem.h
        self.q[eng].append(lambda e, f=fn: f(e).then_inc(h, k))
        return sem.n

    @staticmethod
    def _pick(sem, default):
        if isinstance(sem, Sem):
            return sem
        return default if sem else None

    def set_filler(self, gen):
        self._filler = gen

    def run_filler(self, n=1):
        g = getattr(self, "_filler", None)
        if g is None:
            return
        for _ in range(n):
            try:
                next(g)
            except StopIteration:
                self._filler = None
                return

    def drain_filler(self):
        while getattr(self, "_filler", None) is not None:
            self.run_filler(64)

    def wait(self, eng, sem, val=None):
        if isinstance(sem, tuple):
            sem, val = sem
        if val:
            h = sem.h
            self.q[eng].append(lambda e: e.wait_ge(h, val))

    def dma(self, eng, out, in_, sem, slow=False):
        return self._emit(eng,
                          lambda e, o=out, i=in_, s=slow:
                          e.dma_start(out=o, in_=i, allow_slow_non_contiguous=s),
                          sem, 16)

    def mm(self, out, lhsT, rhs, start, stop, sem=True):
        return self._emit("tensor",
                          lambda e, o=out, l=lhsT, r=rhs, s0=start, s1=stop:
                          e.matmul(o, l, r, start=s0, stop=s1),
                          self._pick(sem, self.pe), 1)

    def transpose(self, out, in_, sem=True):
        ident = self.sb["ident"][:]
        return self._emit("tensor",
                          lambda e, o=out, i=in_, d=ident: e.transpose(o, i, d),
                          self._pick(sem, self.pe), 1)

    def act(self, out, in_, func, bias=0.0, scale=1.0, sem=True):
        return self._emit("scalar",
                          lambda e, o=out, i=in_, f=func, b=bias, s=scale:
                          e.activation(o, i, f, bias=b, scale=s),
                          self._pick(sem, self.ac), 1)

    def tt(self, out, a, b, op, sem=True, eng="vector"):
        sm = self._pick(sem, self.gp if eng == "gpsimd" else self.dv)
        return self._emit(eng,
                          lambda e, o=out, x=a, y=b, p=op: e.tensor_tensor(o, x, y, p),
                          sm, 1)

    def ts2(self, out, in_, s1, s2, op0, op1, sem=True):
        return self._emit("vector",
                          lambda e, o=out, i=in_, a=s1, b=s2, p0=op0, p1=op1:
                          e.tensor_scalar(o, i, a, b, p0, p1),
                          self._pick(sem, self.dv), 1)

    def tss(self, out, in_, s, op, sem=True):
        return self._emit("vector",
                          lambda e, o=out, i=in_, a=s, p=op:
                          e.tensor_single_scalar(o, i, a, p),
                          self._pick(sem, self.dv), 1)

    def stt(self, out, in0, scalar, in1, op0, op1, sem=True):
        return self._emit("vector",
                          lambda e, o=out, i=in0, s=scalar, j=in1, p0=op0, p1=op1:
                          e.scalar_tensor_tensor(o, i, s, j, p0, p1),
                          self._pick(sem, self.dv), 1)

    def copy(self, out, in_, sem=True, eng="vector"):
        sm = self._pick(sem, self.gp if eng == "gpsimd" else self.dv)
        return self._emit(eng, lambda e, o=out, i=in_: e.tensor_copy(o, i), sm, 1)

    def scan(self, out, a, b, init, sem=True):
        return self._emit("vector",
                          lambda e, o=out, x=a, y=b, i=init:
                          e.tensor_tensor_scan(o, x, y, i, OP.mult, OP.add),
                          self.dv if sem else None, 1)

    def recip(self, out, in_, sem=True):
        fb = self.sb["rfence"][:, 0:1]
        v0 = self.copy(fb, in_[:, 0:1])
        self.wait("vector", self.dv, v0)
        v1 = self._emit("vector",
                        lambda e, o=out, i=in_: e.reciprocal(o, i), self.dv, 1)
        self.wait("vector", self.dv, v1)
        return v1

    def redmax(self, out, in_, sem=True):
        return self._emit("vector",
                          lambda e, o=out, i=in_:
                          e.tensor_reduce(o, i, AX.X, OP.max),
                          self.dv if sem else None, 1)

    def memset(self, ap, val, eng="vector", sem=True):
        sm = (self.gp if eng == "gpsimd" else self.dv) if sem else None
        return self._emit(eng, lambda e, a=ap, v=val: e.memset(a, v), sm, 1)

    def fence_loads(self):
        for eng in ("sync", "gpsimd"):
            for s in (self.pe, self.dv, self.ac, self.gp):
                self.wait(eng, s, s.n)

    def barrier(self, exclude=()):
        vals = [(s, s.n) for s in (self.pe, self.dv, self.ac, self.gp)]
        vals += [(s, s.n) for nm, s in self._pool.items()
                 if not any(nm.startswith(p) for p in exclude)]
        for eng in ENGS:
            for s, v in vals:
                self.wait(eng, s, v)

    def run(self, eng, e):
        for f in self.q[eng]:
            f(e)


def _is_dram(ap):
    try:
        return "DRam" in type(ap.tensor).__name__
    except Exception:
        return False


def _rows(ap, r0, n):
    return ap[r0:r0 + n, :]


def _host_layouts(inputs):
    g = {}

    def img(wT, nMo, ktiles, mcols=128):
        K, O = wT.shape
        assert K == 128 * ktiles and O == mcols * nMo
        a = wT.reshape(ktiles, 128, nMo, mcols).transpose(2, 1, 0, 3)
        return np.ascontiguousarray(a.reshape(nMo, 128, ktiles * mcols))

    f32 = lambda a: np.ascontiguousarray(np.asarray(a), dtype=np.float32)

    g["wt_inproj"] = img(f32(inputs["L0_attn_in_proj_w"]).T, 64, 16)
    g["wt_ffproj"] = img(f32(inputs["L0_attn_ff_proj_w"]).T, 64, 16)
    g["wt_outproj"] = img(f32(inputs["L0_attn_out_proj_w"]).T, 16, 32)
    g["wt_ffout"] = img(f32(inputs["L0_attn_ff_out_w"]).T, 16, 32)
    g["wt_gate"] = img(f32(inputs["L0_ffn_w_gate"]).T, 43, 16)
    g["wt_up"] = img(f32(inputs["L0_ffn_w_up"]).T, 43, 16)
    g["wt_down"] = img(f32(inputs["L0_ffn_w_down"]).T, 16, 43)
    xp = f32(inputs["L0_attn_x_proj_w"]).T
    g["wt_xproj"] = np.ascontiguousarray(
        xp.reshape(32, 128, 160).transpose(1, 0, 2).reshape(128, 32 * 160))
    g["wt_dtproj"] = f32(inputs["L0_attn_dt_proj_w"]).T.copy()

    def cols(v, n):
        return np.ascontiguousarray(f32(v).reshape(n, 128).T)

    g["convw_sb"] = np.ascontiguousarray(
        f32(inputs["L0_attn_conv_w"]).reshape(DI, 4).reshape(32, 128, 4)
        .transpose(1, 0, 2).reshape(128, 128))
    g["convb_sb"] = cols(inputs["L0_attn_conv_b"], 32)
    g["dtb_sb"] = cols(inputs["L0_attn_dt_proj_b"], 32)
    g["dsk_sb"] = cols(inputs["L0_attn_D_skip"], 32)
    g["ln1w_sb"] = cols(inputs["L0_attn_ln1_w"], 16)
    g["ln1b_sb"] = cols(inputs["L0_attn_ln1_b"], 16)
    g["ln2w_sb"] = cols(inputs["L0_attn_ln2_w"], 16)
    g["ln2b_sb"] = cols(inputs["L0_attn_ln2_b"], 16)
    al = f32(inputs["L0_attn_A_log"])
    g["alog_sb"] = np.ascontiguousarray(
        al.reshape(512, 8, 16).transpose(1, 2, 0).reshape(128, 512))
    g["ones_sb"] = np.ones((128, 128), dtype=np.float32)
    g["ident_sb"] = np.eye(128, dtype=np.float32)
    g["kconst_sb"] = np.tile(np.array([[EPS, -0.5, 1.0, 0.0]], dtype=np.float32),
                             (128, 1))
    psel = np.zeros((128, 16 * 128), dtype=np.float32)
    for kk in range(16):
        for p in range(128):
            psel[p, 128 * kk + 8 * kk + p // 16] = 1.0
    g["psel_sb"] = psel
    return g


def build_program(debug_taps=()):
    nc = bass.Bass("TRN2", target_bir_lowering=False,
                   detect_race_conditions=False)
    dram = {}

    def din(name, shape, dtype=F32):
        dram[name] = nc.dram_tensor(name, list(shape), dtype, kind="ExternalInput").ap()

    def dscr(name, shape, dtype=F32):
        kind = "Internal" if name not in debug_taps else "ExternalOutput"
        dram[name] = nc.dram_tensor(name, list(shape), dtype, kind=kind).ap()

    din("x_f", (D, T))
    din("wt_inproj", (64, 128, 2048)); din("wt_ffproj", (64, 128, 2048))
    din("wt_outproj", (16, 128, 4096)); din("wt_ffout", (16, 128, 4096))
    din("wt_gate", (43, 128, 2048)); din("wt_up", (43, 128, 2048))
    din("wt_down", (16, 128, 5504))
    din("wt_xproj", (128, 32 * 160)); din("wt_dtproj", (128, 4096))
    for nm, n in [("convw_sb", 128), ("convb_sb", 32), ("dtb_sb", 32),
                  ("dsk_sb", 32), ("ln1w_sb", 16), ("ln1b_sb", 16),
                  ("ln2w_sb", 16), ("ln2b_sb", 16), ("alog_sb", 512),
                  ("ones_sb", 128), ("ident_sb", 128), ("psel_sb", 16 * 128),
                  ("kconst_sb", 4)]:
        din(nm, (128, n))
    dram["out_f"] = nc.dram_tensor("out_f", [D, T], F32, kind="ExternalOutput").ap()

    dscr("oph", (16, 128, 4096), BF16); dscr("opl", (16, 128, 4096), BF16)
    dscr("foh", (16, 128, 4096), BF16); dscr("fol", (16, 128, 4096), BF16)
    dscr("wqgh", (43, 128, 2048), BF16); dscr("wqgl", (43, 128, 2048), BF16)
    dscr("wquh", (43, 128, 2048), BF16); dscr("wqul", (43, 128, 2048), BF16)
    dscr("wqdh", (16, 128, 5504), BF16); dscr("wqdl", (16, 128, 5504), BF16)
    dscr("xnh", (D, T), BF16); dscr("xnl", (D, T), BF16)
    dscr("u1h", (D, T), BF16); dscr("u1l", (D, T), BF16)
    dscr("qb", (D, T), BF16)
    dscr("sflat", (1, T)); dscr("rflat", (1, T))
    dscr("xc", (DI, T)); dscr("sz", (DI, T)); dscr("dt", (DI, T)); dscr("uu", (DI, T))
    dscr("bcflat", (32, T))
    dscr("hid", (DFF, T))
    dscr("s2flat", (1, T)); dscr("r2flat", (1, T))
    dscr("hqb", (DFF, T), BF16)
    dscr("dwn", (D, T))
    dscr("ygh", (DI, T), BF16); dscr("ygl", (DI, T), BF16)
    dscr("anx", (D, T))
    dscr("ln2h", (D, T), BF16); dscr("ln2l", (D, T), BF16)
    dscr("hgh", (DI, T), BF16); dscr("hgl", (DI, T), BF16)

    import contextlib
    stack = contextlib.ExitStack()
    sb, ps = {}, {}

    def sbuf(name, shape, dtype=F32):
        sb[name] = stack.enter_context(nc.sbuf_tensor(name, list(shape), dtype))

    with stack:
        for nm, n in [("convw", 128), ("convb", 32), ("dtb", 32), ("dsk", 32),
                      ("ln1w", 16), ("ln1b", 16), ("ln2w", 16), ("ln2b", 16),
                      ("ones", 128), ("ident", 128), ("kconst", 4),
                      ("rfence", 8)]:
            sbuf(nm, (128, n))
        for ci, cv in enumerate([EPS, -0.5, 1.0, 0.0]):
            nc.const_aps.register(F32, cv, sb["kconst"][:, ci:ci + 1])
        sbuf("alog", (128, 512)); sbuf("aneg", (128, 512))
        sbuf("s1w", (128, 8 * 512)); sbuf("s1b", (128, 7 * 512), BF16)
        sbuf("s2b", (128, T)); sbuf("r2b", (128, T))
        for i in range(12):
            sbuf(f"wb{i}", (128, 512))
        for i in range(8):
            sbuf(f"bb{i}", (128, 512), BF16)
        sbuf("halo", (128, 3 * 32))
        for i in range(8):
            ps[f"pb{i}"] = stack.enter_context(
                nc.psum_tensor(f"pb{i}", [128, 512], F32))

        sems = {nm: Sem(stack.enter_context(nc.semaphore(nm)))
                for nm in ("pe", "dv", "ac", "gp")}
        pool_handles = [stack.enter_context(nc.semaphore(f"sp{i}"))
                        for i in range(96)]

        abase = (nc.sbuf_base + 31) & ~31
        c = Ctx(nc, dram, sb, ps, sems, pool_handles,
                Arena(nc, abase, nc.sbuf_top))

        emit_s0(c); c.barrier()
        c.set_filler(emit_s1(c))
        emit_s2(c); c.barrier(exclude=("s1",))
        emit_s3(c); c.barrier(exclude=("s1",))
        emit_s3b(c)
        c.drain_filler()
        c.barrier()
        emit_s4(c); c.barrier()
        c.set_filler(emit_s5(c))
        emit_s6(c)
        c.drain_filler()
        c.barrier()
        emit_s7(c); c.barrier()
        emit_s8(c); c.barrier()
        emit_s9(c); c.barrier()

        with nc.Block() as block:
            @block.sync
            def _(e):
                c.run("sync", e)

            @block.tensor
            def _(e):
                c.run("tensor", e)

            @block.vector
            def _(e):
                c.run("vector", e)

            @block.scalar
            def _(e):
                c.run("scalar", e)

            @block.gpsimd
            def _(e):
                c.run("gpsimd", e)

    return nc


def pair_split(c, src_f32, hi_bf, lo_bf, gsem):
    v = c.dma("gpsimd", hi_bf, src_f32, gsem)
    c.wait("vector", gsem, v)
    return c.tt(lo_bf, src_f32, hi_bf, OP.subtract)


def ladder_round_clip(c, v, t0, tbuf, dbuf, qout):
    c.ts2(v, v, -128.0, 127.0, OP.max, OP.min, sem=False)
    c.tss(t0, v, 128.5, OP.add, sem=False)
    c.copy(tbuf, t0, sem=False)
    for B in (128.0, 64.0, 32.0, 16.0, 8.0, 4.0, 2.0, 1.0):
        c.ts2(dbuf, tbuf, B, -B, OP.is_ge, OP.mult, sem=False)
        c.tt(tbuf, tbuf, dbuf, OP.add, sem=False)
    c.tt(t0, t0, tbuf, OP.subtract, sem=False)
    return c.tss(qout, t0, -128.0, OP.add)


def ln_block(c, src_buf, ntiles, tcw, pb_m, pb_q, sqb, wait_sem, wait_val,
             pe_sq):
    ps, sb = c.ps, c.sb
    c.wait("scalar", wait_sem, wait_val)
    c.wait("tensor", wait_sem, wait_val)
    for i in range(ntiles):
        v_sq = c.act(sqb[:, tcw * i:tcw * (i + 1)],
                     src_buf[:, tcw * i:tcw * (i + 1)], AF.Square)
        xi = src_buf[:, tcw * i:tcw * (i + 1)]
        for n0, nw in spans(tcw):
            c.mm(ps[pb_m][:, n0:n0 + nw], sb["ones"][:], xi[:, n0:n0 + nw],
                 start=(i == 0), stop=(i == ntiles - 1),
                 sem=(i == ntiles - 1))
        c.wait("tensor", c.ac, v_sq)
        for n0, nw in spans(tcw):
            c.mm(ps[pb_q][:, n0:n0 + nw], sb["ones"][:],
                 sqb[:, tcw * i:tcw * (i + 1)][:, n0:n0 + nw],
                 start=(i == 0), stop=(i == ntiles - 1),
                 sem=(i == ntiles - 1))
    return c.pe.n


def ln_finish(c, pb_m, pb_q, tcw, m_b, var, r_b, tmp):
    ps = c.ps
    c.wait("vector", c.pe, c.pe.n)
    c.tss(m_b, ps[pb_m][:, :tcw], 1.0 / D, OP.mult, sem=False)
    c.tss(var, ps[pb_q][:, :tcw], 1.0 / D, OP.mult, sem=False)
    c.tt(tmp, m_b, m_b, OP.mult, sem=False)
    v_dv = c.tt(var, var, tmp, OP.subtract)
    c.wait("scalar", c.dv, v_dv)
    c.act(tmp, var, AF.Ln, bias=EPS)
    v_ac = c.act(r_b, tmp, AF.Exp, scale=-0.5)
    c.wait("vector", c.ac, v_ac)


def emit_s0(c):
    sb, dram, ar = c.sb, c.dram, c.ar
    ar.reset()
    tgt = {"convw_sb": "convw", "convb_sb": "convb", "dtb_sb": "dtb",
           "dsk_sb": "dsk", "ln1w_sb": "ln1w", "ln1b_sb": "ln1b",
           "ln2w_sb": "ln2w", "ln2b_sb": "ln2b", "ones_sb": "ones",
           "ident_sb": "ident", "alog_sb": "alog",
           "kconst_sb": "kconst"}
    s_res = c.s("s0res")
    for src, dst in tgt.items():
        c.dma("sync", sb[dst][:], dram[src][:], s_res)
    v_res = s_res.n

    c.wait("scalar", s_res, v_res)
    c.act(sb["aneg"][:], sb["alog"][:], AF.Exp)
    v_exp = c.ac.n
    c.wait("vector", c.ac, v_exp)
    c.tss(sb["aneg"][:], sb["aneg"][:], -1.0, OP.mult)

    blk = [ar.f32(4096), ar.f32(4096)]
    blkh = [ar.bf16(4096), ar.bf16(4096)]
    blkl = [ar.bf16(4096), ar.bf16(4096)]
    s_ld = [c.s("s0ldA"), c.s("s0ldB")]
    s_g = [c.s("s0gA"), c.s("s0gB")]
    s_st = [c.s("s0stA"), c.s("s0stB")]
    st_done = {}
    it = 0
    for src, dh, dl in [("wt_outproj", "oph", "opl"), ("wt_ffout", "foh", "fol")]:
        for mo in range(16):
            r = it % 2
            if it >= 2:
                c.wait("sync", s_st[r], st_done[it - 2])
            v_ld = c.dma("sync", blk[r], dram[src][mo, :, :], s_ld[r])
            c.wait("gpsimd", s_ld[r], v_ld)
            if it >= 2:
                c.wait("gpsimd", s_st[r], st_done[it - 2])
            v_hi = c.dma("gpsimd", blkh[r], blk[r], s_g[r])
            c.wait("vector", s_g[r], v_hi)
            v_lo = c.tt(blkl[r], blk[r], blkh[r], OP.subtract)
            c.wait("sync", c.dv, v_lo)
            c.dma("sync", dram[dh][mo, :, :], blkh[r], s_st[r])
            c.dma("sync", dram[dl][mo, :, :], blkl[r], s_st[r])
            st_done[it] = s_st[r].n
            it += 1


def emit_s1(c):
    sb, ps, dram = c.sb, c.ps, c.dram
    limit = int(_os.environ.get("K_S1_LIMIT", "0")) or None
    cfgs = [("wt_gate", "wqgh", "wqgl", 43, 16),
            ("wt_up", "wquh", "wqul", 43, 16),
            ("wt_down", "wqdh", "wqdl", 16, 43)]
    s_ld = [c.s("s1ldA"), c.s("s1ldB")]
    s_st = [c.s("s1stA"), c.s("s1stB")]
    pe1, dv1, ac1 = c.s("s1pe"), c.s("s1dv"), c.s("s1ac")
    s_res = c.s("s0res")
    v_res = s_res.n
    first = True
    W = lambda i: sb["s1w"][:, 512 * i:512 * (i + 1)]
    B = lambda i: sb["s1b"][:, 512 * i:512 * (i + 1)]
    st_done, dv_done, pe_done = {}, {}, {}
    it = 0
    for srcn, dsth, dstl, nMo, nK in cfgs:
        for mo in range(nMo):
            for k4 in range(0, nK, 4):
                if limit is not None and it >= limit:
                    return
                cw = min(4, nK - k4) * 128
                r = it % 2
                wt = W(r)[:, :cw]
                ab = W(2 + r)[:, :cw]
                pb = ps[f"pb{6 + r}"][:, :cw]
                wqh = B(2 + r)[:, :cw]
                wql = B(4 + r)[:, :cw]
                if it >= 2:
                    c.wait("sync", dv1, dv_done[it - 2])
                v_ld = c.dma("sync", wt,
                             dram[srcn][mo, :, 128 * k4:128 * k4 + cw], s_ld[r])
                c.wait("scalar", s_ld[r], v_ld)
                if first:
                    c.wait("scalar", s_res, v_res)
                    c.wait("tensor", s_res, v_res)
                    first = False
                if it >= 2:
                    c.wait("scalar", pe1, pe_done[it - 2])
                    c.wait("scalar", dv1, dv_done[it - 2])
                c.act(ab, wt, AF.Abs, sem=ac1)
                v_abs = ac1.n
                c.wait("tensor", ac1, v_abs)
                if it >= 2:
                    c.wait("tensor", dv1, dv_done[it - 2])
                pe_done[it] = c.mm(pb, sb["ones"][:], ab, True, True, sem=pe1)
                c.wait("vector", pe1, pe_done[it])
                if it >= 2:
                    c.wait("vector", s_st[r], st_done[it - 2])
                sc = W(4)[:, :cw]
                hs = W(5)[:, :cw]
                hsn = W(6)[:, :cw]
                q1 = W(7)[:, :cw]
                q2 = ab
                tern, shi, slo = B(0)[:, :cw], B(1)[:, :cw], B(6)[:, :cw]
                c.ts2(sc, pb, 1.0 / 128.0, EPS, OP.mult, OP.add, sem=False)
                c.tss(hs, sc, 0.5, OP.mult, sem=False)
                c.tss(hsn, sc, -0.5, OP.mult, sem=False)
                c.tt(q1, wt, hs, OP.is_ge, sem=False)
                c.tt(q2, wt, hsn, OP.is_le, sem=False)
                c.tt(tern, q1, q2, OP.subtract, sem=False)
                c.copy(shi, sc, sem=False)
                c.tt(slo, sc, shi, OP.subtract, sem=False)
                c.tt(wqh, tern, shi, OP.mult, sem=False)
                dv_done[it] = c.tt(wql, tern, slo, OP.mult, sem=dv1)
                c.wait("sync", dv1, dv_done[it])
                c.dma("sync", dram[dsth][mo, :, 128 * k4:128 * k4 + cw], wqh,
                      s_st[r])
                c.dma("sync", dram[dstl][mo, :, 128 * k4:128 * k4 + cw], wql,
                      s_st[r])
                st_done[it] = s_st[r].n
                it += 1
                yield


def emit_s2(c):
    sb, ps, dram, ar = c.sb, c.ps, c.dram, c.ar
    ar.reset()
    xch = ar.f32(16 * TC)
    xnb = ar.f32(16 * TC)
    sbb = ar.f32(TC)
    sqb = ar.f32(16 * TC)

    s_x = [c.s("s2xA"), c.s("s2xB")]
    s_st = c.s("s2st")
    s_g = c.s("s2g")
    for tc in range(NTC):
        col = tc * TC
        c.fence_loads()
        sx = s_x[tc % 2]
        for i in range(16):
            c.dma("sync", xch[:, TC * i:TC * (i + 1)],
                  dram["x_f"][128 * i:128 * (i + 1), col:col + TC], sx)
        v_ld = sx.n

        pe_sq = {}
        ln_block(c, xch, 16, TC, "pb0", "pb1", sqb, sx, v_ld, pe_sq)
        m_b, var = sb["wb0"][:, :TC], sb["wb1"][:, :TC]
        r_b, tmp = sb["wb2"][:, :TC], sb["wb3"][:, :TC]
        ln_finish(c, "pb0", "pb1", TC, m_b, var, r_b, tmp)
        for i in range(16):
            c.run_filler(2)
            xi = xch[:, TC * i:TC * (i + 1)]
            xn = xnb[:, TC * i:TC * (i + 1)]
            c.tt(xn, xi, m_b, OP.subtract, sem=False)
            c.tt(xn, xn, r_b, OP.mult, sem=(i == 15))
        v_xn = c.dv.n

        pe_sq = {}
        ln_block(c, xnb, 16, TC, "pb2", "pb3", sqb, c.dv, v_xn, pe_sq)
        m1, var1 = sb["wb4"][:, :TC], sb["wb5"][:, :TC]
        r1 = sb["wb6"][:, :TC]
        ln_finish(c, "pb2", "pb3", TC, m1, var1, r1, tmp)
        t1 = sb["wb7"][:, :TC]
        u1f = sb["wb8"][:, :TC]
        st_i = {}
        for i in range(16):
            c.run_filler(2)
            rr = i % 2
            xn = xnb[:, TC * i:TC * (i + 1)]
            c.tt(t1, xn, m1, OP.subtract, sem=False)
            v_dv = c.tt(t1, t1, r1, OP.mult)
            c.wait("scalar", c.dv, v_dv)
            v_dv = c.stt(u1f, t1, sb["ln1w"][:, i:i + 1],
                         sb["ln1b"][:, i:i + 1].broadcast_to([128, TC]),
                         OP.mult, OP.add)
            uh, ul = sb[f"bb{rr}"][:, :TC], sb[f"bb{2 + rr}"][:, :TC]
            xh, xl = sb[f"bb{4 + rr}"][:, :TC], sb[f"bb{6 + rr}"][:, :TC]
            if i >= 2:
                c.wait("vector", s_st, st_i[i - 2])
            c.copy(uh, u1f, sem=False)
            v_dv = c.tt(ul, u1f, uh, OP.subtract)
            c.wait("sync", c.dv, v_dv)
            c.dma("sync", dram["u1h"][128 * i:128 * (i + 1), col:col + TC], uh,
                  s_st)
            c.dma("sync", dram["u1l"][128 * i:128 * (i + 1), col:col + TC], ul,
                  s_st)
            c.copy(xh, xn, sem=False)
            v_dv = c.tt(xl, xn, xh, OP.subtract)
            c.wait("sync", c.dv, v_dv)
            c.dma("sync", dram["xnh"][128 * i:128 * (i + 1), col:col + TC], xh,
                  s_st)
            v_st = c.dma("sync",
                         dram["xnl"][128 * i:128 * (i + 1), col:col + TC], xl,
                         s_st)
            st_i[i] = v_st

        acc = sb["wb9"][:, :TC]
        for i in range(16):
            c.run_filler(2)
            ab = sqb[:, TC * (i % 2):TC * (i % 2) + TC]
            if i >= 2:
                c.wait("scalar", c.dv, dv_ab[i - 2])
            v_ab = c.act(ab, xnb[:, TC * i:TC * (i + 1)], AF.Abs)
            c.wait("vector", c.ac, v_ab)
            if i == 0:
                dv_ab = {}
                dv_ab[i] = c.copy(acc, ab)
            else:
                dv_ab[i] = c.tt(acc, acc, ab, OP.max)
        v_acc = c.dv.n
        c.wait("tensor", c.dv, v_acc)
        atok = sb["wb10"][:, :TC // 128]
        for cb in range(TC // 128):
            v_tp = c.transpose(ps["pb4"][:, :128], acc[:, 128 * cb:128 * (cb + 1)])
            c.wait("vector", c.pe, v_tp)
            v_rm = c.redmax(atok[:, cb:cb + 1], ps["pb4"][:, :128])
            c.wait("tensor", c.dv, v_rm)
        c.wait("vector", c.dv, c.dv.n)
        stok = sb["wb11"][:, :TC // 128]
        rtok = sb["wb9"][:, :TC // 128]
        c.tss(atok, atok, EPS, OP.max, sem=False)
        c.recip(stok, atok, sem=False)
        c.tss(stok, stok, 127.0, OP.mult, sem=False)
        v_dv = c.recip(rtok, stok)
        c.wait("sync", c.dv, v_dv)
        sf = dram["sflat"][0, col:col + TC].rearrange("(cc p) -> p cc", p=128)
        rf = dram["rflat"][0, col:col + TC].rearrange("(cc p) -> p cc", p=128)
        c.dma("sync", sf, stok, s_st, slow=True)
        v_st = c.dma("sync", rf, rtok, s_st, slow=True)
        c.wait("gpsimd", s_st, v_st)
        for sl in range(8):
            v_gd = c.dma("gpsimd", _rows(sbb, 16 * sl, 16),
                         dram["sflat"][0:1, col:col + TC].broadcast_to([16, TC]),
                         s_g)
        c.wait("vector", s_g, v_gd)

        v = sb["wb7"][:, :TC]
        t0 = sb["wb8"][:, :TC]
        tb = sb["wb10"][:, :TC]
        db = sb["wb11"][:, :TC]
        st_q = {}
        for i in range(16):
            c.run_filler(2)
            xn = xnb[:, TC * i:TC * (i + 1)]
            c.tt(v, xn, sbb, OP.mult, sem=False)
            qt = sb[f"bb{4 + i % 2}"][:, :TC]
            if i >= 2:
                c.wait("vector", s_st, st_q[i - 2])
            v_dv = ladder_round_clip(c, v, t0, tb, db, qt)
            c.wait("sync", c.dv, v_dv)
            st_q[i] = c.dma("sync",
                            dram["qb"][128 * i:128 * (i + 1), col:col + TC], qt,
                            s_st)


def emit_s3(c):
    sb, ps, dram, ar = c.sb, c.ps, c.dram, c.ar
    ar.reset()
    u1h = ar.bf16(16 * TC2)
    u1l = ar.bf16(16 * TC2)
    blkw = [ar.f32(2048), ar.f32(2048)]
    blkwh = [ar.bf16(2048), ar.bf16(2048)]
    blkwl = [ar.bf16(2048), ar.bf16(2048)]
    xmb = [ar.f32(TC2 + 3), ar.f32(TC2 + 3)]
    cvb = [ar.f32(TC2), ar.f32(TC2)]
    xct = [ar.f32(TC2), ar.f32(TC2)]

    s_u = [c.s("s3uA"), c.s("s3uB")]
    s_w = [c.s("s3wA"), c.s("s3wB")]
    s_g = [c.s("s3gA"), c.s("s3gB")]
    s_st = c.s("s3st")

    def split(mo, sub_lo, pe_guard):
        r = mo % 2
        if sub_lo[mo - 2] is not None:
            c.wait("sync", c.dv, sub_lo[mo - 2])
        v_w = c.dma("sync", blkw[r], dram["wt_inproj"][mo, :, :], s_w[r])
        c.wait("gpsimd", s_w[r], v_w)
        if pe_guard is not None:
            c.wait("gpsimd", c.pe, pe_guard)
        v_hi = c.dma("gpsimd", blkwh[r], blkw[r], s_g[r])
        c.wait("vector", s_g[r], v_hi)
        if pe_guard is not None:
            c.wait("vector", c.pe, pe_guard)
        sub_lo[mo] = c.tt(blkwl[r], blkw[r], blkwh[r], OP.subtract)

    for tc2 in range(NT2):
        col = tc2 * TC2
        c.fence_loads()
        su = s_u[tc2 % 2]
        for i in range(16):
            c.dma("sync", u1h[:, TC2 * i:TC2 * (i + 1)],
                  dram["u1h"][128 * i:128 * (i + 1), col:col + TC2], su)
            c.dma("sync", u1l[:, TC2 * i:TC2 * (i + 1)],
                  dram["u1l"][128 * i:128 * (i + 1), col:col + TC2], su)
        v_u1 = su.n
        c.wait("tensor", su, v_u1)

        import collections
        sub_lo = collections.defaultdict(lambda: None)
        pe_blk, st_done, conv_done, silu_ac, evac_done = {}, {}, {}, {}, {}
        split(0, sub_lo, None)
        for mo in range(64):
            c.run_filler(7)
            r = mo % 2
            if mo + 1 < 64:
                split(mo + 1, sub_lo, pe_blk.get(mo - 1))
            c.wait("tensor", c.dv, sub_lo[mo])
            if mo >= 2:
                c.wait("tensor", evac_done[mo - 2])
            bs = 2 * (mo % 2)
            v_mm = None
            for n0, nw in spans(TC2):
                pbk = ps[f"pb{bs + (n0 // 512) % 2}"]
                for k in range(16):
                    wh = blkwh[r][:, 128 * k:128 * (k + 1)]
                    wl = blkwl[r][:, 128 * k:128 * (k + 1)]
                    uh = u1h[:, TC2 * k + n0:TC2 * k + n0 + nw]
                    ul = u1l[:, TC2 * k + n0:TC2 * k + n0 + nw]
                    c.mm(pbk[:, :nw], wh, uh, start=(k == 0), stop=False,
                         sem=False)
                    c.mm(pbk[:, :nw], wh, ul, start=False, stop=False, sem=False)
                    v_mm = c.mm(pbk[:, :nw], wl, uh, start=False, stop=(k == 15))
            pe_blk[mo] = c.pe.n
            if mo < 32:
                c.wait("scalar", c.pe, v_mm)
                if mo >= 2:
                    c.wait("scalar", c.dv, conv_done[mo - 2])
                v_ev = None
                for n0, nw in spans(TC2):
                    v_ev = c.act(xmb[r][:, 3 + n0:3 + n0 + nw],
                                 ps[f"pb{bs + (n0 // 512) % 2}"][:, :nw], AF.Copy)
                evac_done[mo] = (c.ac, v_ev)
                c.wait("vector", c.ac, v_ev)
                if mo >= 2:
                    c.wait("vector", c.ac, silu_ac[mo - 2])
                if tc2 == 0:
                    c.memset(xmb[r][:, 0:3], 0.0, sem=False)
                else:
                    c.copy(xmb[r][:, 0:3], sb["halo"][:, 3 * mo:3 * mo + 3],
                           sem=False)
                c.ts2(cvb[r], xmb[r][:, 0:TC2], sb["convw"][:, 4 * mo:4 * mo + 1],
                      None, OP.mult, OP.bypass, sem=False)
                for kk in range(1, 4):
                    c.stt(cvb[r], xmb[r][:, kk:kk + TC2],
                          sb["convw"][:, 4 * mo + kk:4 * mo + kk + 1], cvb[r],
                          OP.mult, OP.add, sem=False)
                conv_done[mo] = c.copy(sb["halo"][:, 3 * mo:3 * mo + 3],
                                       xmb[r][:, TC2:TC2 + 3])
                c.wait("scalar", c.dv, conv_done[mo])
                if mo >= 2:
                    c.wait("scalar", s_st, st_done[mo - 2])
                silu_ac[mo] = c.act(xct[r], cvb[r], AF.Silu,
                                    bias=sb["convb"][:, mo:mo + 1])
                c.wait("sync", c.ac, silu_ac[mo])
                st_done[mo] = c.dma(
                    "sync", dram["xc"][128 * mo:128 * (mo + 1), col:col + TC2],
                    xct[r], s_st)
            else:
                zi = mo - 32
                c.wait("scalar", c.pe, v_mm)
                if mo >= 2:
                    c.wait("scalar", s_st, st_done[mo - 2])
                v_ac = None
                for n0, nw in spans(TC2):
                    v_ac = c.act(xct[r][:, n0:n0 + nw],
                                 ps[f"pb{bs + (n0 // 512) % 2}"][:, :nw], AF.Silu)
                silu_ac[mo] = v_ac
                conv_done[mo] = c.dv.n
                evac_done[mo] = (c.ac, v_ac)
                c.wait("sync", c.ac, v_ac)
                st_done[mo] = c.dma(
                    "sync", dram["sz"][128 * zi:128 * (zi + 1), col:col + TC2],
                    xct[r], s_st)


def emit_s3b(c):
    sb, ps, dram, ar = c.sb, c.ps, c.dram, c.ar
    ar.reset()
    xpjh = ar.bf16(32 * 160)
    xpjl = ar.bf16(32 * 160)
    dtpjh = ar.bf16(4096)
    dtpjl = ar.bf16(4096)
    m0 = ar.mark()
    xpjf = ar.f32(32 * 160)
    dtpjf = ar.f32(4096)
    s_pj = c.s("s3bpj")
    c.dma("sync", xpjf, dram["wt_xproj"][:], s_pj)
    v_pj = c.dma("sync", dtpjf, dram["wt_dtproj"][:], s_pj)
    c.wait("gpsimd", s_pj, v_pj)
    pair_split(c, xpjf, xpjh, xpjl, c.s("s3bpg"))
    v_pj2 = pair_split(c, dtpjf, dtpjh, dtpjl, c.s("s3bpg2"))
    c.wait("tensor", c.dv, v_pj2)
    ar.rewind(m0)
    xcch = ar.bf16(32 * TC)
    xccl = ar.bf16(32 * TC)
    xcb4 = [ar.f32(4 * TC), ar.f32(4 * TC)]
    dtlf = ar.f32(TC)
    dtlh = ar.bf16(TC)
    dtll = ar.bf16(TC)
    dtt = sb["wb5"][:, :TC]
    ut = sb["wb6"][:, :TC]

    for tc in range(NTC):
        col = tc * TC
        c.fence_loads()
        dv_sp = {}
        s_x = [c.s("s3bxA"), c.s("s3bxB")]
        s_g = [c.s("s3bgA"), c.s("s3bgB")]
        s_st = c.s("s3bst")
        for jb in range(8):
            c.run_filler(2)
            r = jb % 2
            if jb >= 2:
                c.wait("sync", c.dv, dv_sp[jb - 2])
            v_ld = None
            for u in range(4):
                j = 4 * jb + u
                v_ld = c.dma("sync", xcb4[r][:, TC * u:TC * (u + 1)],
                             dram["xc"][128 * j:128 * (j + 1), col:col + TC],
                             s_x[r])
            c.wait("gpsimd", s_x[r], v_ld)
            v_hi = c.dma("gpsimd", xcch[:, 4 * TC * jb:4 * TC * (jb + 1)],
                         xcb4[r], s_g[r])
            c.wait("vector", s_g[r], v_hi)
            dv_sp[jb] = c.tt(xccl[:, 4 * TC * jb:4 * TC * (jb + 1)], xcb4[r],
                             xcch[:, 4 * TC * jb:4 * TC * (jb + 1)], OP.subtract)
        c.wait("tensor", c.dv, c.dv.n)

        v_mm = None
        for k in range(32):
            st0, sp0 = (k == 0), (k == 31)
            ph = xpjh[:, 160 * k:160 * k + 128]
            pl = xpjl[:, 160 * k:160 * k + 128]
            ph2 = xpjh[:, 160 * k + 128:160 * (k + 1)]
            pl2 = xpjl[:, 160 * k + 128:160 * (k + 1)]
            for n0, nw in spans(TC):
                xh = xcch[:, TC * k + n0:TC * k + n0 + nw]
                xl = xccl[:, TC * k + n0:TC * k + n0 + nw]
                c.mm(ps["pb0"][:, :nw], ph, xh, start=st0, stop=False, sem=False)
                c.mm(ps["pb0"][:, :nw], ph, xl, start=False, stop=False, sem=False)
                c.mm(ps["pb0"][:, :nw], pl, xh, start=False, stop=sp0, sem=sp0)
                c.mm(ps["pb1"][:32, :nw], ph2, xh, start=st0, stop=False, sem=False)
                c.mm(ps["pb1"][:32, :nw], ph2, xl, start=False, stop=False,
                     sem=False)
                v_mm = c.mm(ps["pb1"][:32, :nw], pl2, xh, start=False, stop=sp0,
                            sem=sp0)
        c.wait("vector", c.pe, c.pe.n)
        c.copy(dtlf, ps["pb0"][:, :TC], sem=False)
        c.copy(sb["wb0"][:32, :TC], ps["pb1"][:32, :TC], sem=False)
        c.copy(dtlh, dtlf, sem=False)
        v_dv = c.tt(dtll, dtlf, dtlh, OP.subtract)
        c.wait("sync", c.dv, v_dv)
        c.dma("sync", dram["bcflat"][:, col:col + TC], sb["wb0"][:32, :TC], s_st)

        c.wait("tensor", c.dv, v_dv)
        for mo in range(32):
            c.run_filler(2)
            r = mo % 2
            wh = dtpjh[:, 128 * mo:128 * (mo + 1)]
            wl = dtpjl[:, 128 * mo:128 * (mo + 1)]
            pbk = ps[f"pb{2 + mo % 2}"]
            if mo >= 2:
                c.wait("sync", c.dv, dv_sp[mo - 2 + 100])
            v_xc = c.dma("sync", xcb4[r][:, :TC],
                         dram["xc"][128 * mo:128 * (mo + 1), col:col + TC],
                         s_x[r])
            v_mm = None
            for n0, nw in spans(TC):
                c.mm(pbk[:, :nw], wh, dtlh[:, n0:n0 + nw], start=True, stop=False,
                     sem=False)
                c.mm(pbk[:, :nw], wh, dtll[:, n0:n0 + nw], start=False, stop=False,
                     sem=False)
                v_mm = c.mm(pbk[:, :nw], wl, dtlh[:, n0:n0 + nw], start=False,
                            stop=True)
            c.wait("vector", c.pe, v_mm)
            c.ts2(dtt, pbk[:, :TC], sb["dtb"][:, mo:mo + 1], 80.0, OP.add, OP.min,
                  sem=False)
            v_dv = c.copy(dtt, dtt)
            c.wait("scalar", c.dv, v_dv)
            c.act(dtt, dtt, AF.Exp, sem=False)
            v_ac = c.act(dtt, dtt, AF.Ln, bias=1.0)
            c.wait("vector", c.ac, v_ac)
            c.wait("vector", s_x[r], v_xc)
            v_dv = c.tt(ut, dtt, xcb4[r][:, :TC], OP.mult)
            dv_sp[mo + 100] = v_dv
            c.wait("sync", c.ac, v_ac)
            c.wait("sync", c.dv, v_dv)
            c.dma("sync", dram["dt"][128 * mo:128 * (mo + 1), col:col + TC], dtt,
                  s_st)
            v_st = c.dma("sync",
                         dram["uu"][128 * mo:128 * (mo + 1), col:col + TC], ut,
                         s_st)
            c.wait("vector", s_st, v_st)
            c.wait("scalar", s_st, v_st)
            c.wait("tensor", c.dv, v_dv)


def emit_s4(c):
    sb, ps, dram, ar = c.sb, c.ps, c.dram, c.ar
    ar.reset()
    rsb = ar.f32(T)
    wqh = [ar.bf16(2048), ar.bf16(2048)]
    wql = [ar.bf16(2048), ar.bf16(2048)]
    sgb = ar.f32(T)
    hidt = [c.sb["wb1"][:, :512], c.sb["wb4"][:, :512]]
    qmark = ar.mark()
    qbuf = ar.bf16(16 * T)

    s_q = c.s("s4q")
    s_g = c.s("s4g")
    s_w = [c.s("s4wA"), c.s("s4wB")]
    s_st = c.s("s4st")
    s_h = [c.s("s4hA"), c.s("s4hB")]
    for i in range(16):
        c.dma("sync", qbuf[:, T * i:T * (i + 1)],
              dram["qb"][128 * i:128 * (i + 1), :], s_q)
    v_gd = None
    for sl in range(8):
        v_gd = c.dma("gpsimd", _rows(rsb, 16 * sl, 16),
                     dram["rflat"][0:1, :].broadcast_to([16, T]), s_g)
    v_q = s_q.n
    c.wait("tensor", s_q, v_q)
    c.wait("vector", s_g, v_gd)

    pe_done, ev_gate, ev_up, st_hid = {}, {}, {}, {}
    for mo in range(43):
        for br, (srch, srcl, pboff) in enumerate(
                [("wqgh", "wqgl", 0), ("wquh", "wqul", 4)]):
            r = br
            if mo >= 1:
                c.wait("sync", c.pe, pe_done[(mo - 1, br)])
            c.dma("sync", wqh[r], dram[srch][mo, :, :], s_w[r])
            v_w = c.dma("sync", wql[r], dram[srcl][mo, :, :], s_w[r])
            c.wait("tensor", s_w[r], v_w)
            for n0, nw in spans(T):
                ni = n0 // 512
                pbk = ps[f"pb{pboff + ni}"]
                if mo >= 1:
                    c.wait("tensor", (ev_gate if br == 0 else ev_up)
                           .get((mo - 1, ni), (c.dv, 0)))
                v_mm = None
                for k in range(16):
                    qk = qbuf[:, T * k + n0:T * k + n0 + nw]
                    c.mm(pbk[:, :nw], wqh[r][:, 128 * k:128 * (k + 1)], qk,
                         start=(k == 0), stop=False, sem=False)
                    v_mm = c.mm(pbk[:, :nw], wql[r][:, 128 * k:128 * (k + 1)], qk,
                                start=False, stop=(k == 15))
                c.wait("vector", c.pe, v_mm)
                if br == 0:
                    v_dv = c.tt(sb["wb0"][:, :nw], pbk[:, :nw],
                                rsb[:, n0:n0 + nw], OP.mult)
                    c.wait("scalar", c.dv, v_dv)
                    v_ac = c.act(sgb[:, n0:n0 + nw], sb["wb0"][:, :nw], AF.Silu)
                    ev_gate[(mo, ni)] = (c.ac, v_ac)
                else:
                    c.wait("vector", ev_gate[(mo, ni)])
                    hidx = mo * (T // 512) + ni
                    hb = hidt[hidx % 2]
                    if hidx >= 2:
                        c.wait("vector", s_st, st_hid[hidx - 2])
                    c.tt(sb["wb0"][:, :nw], pbk[:, :nw], rsb[:, n0:n0 + nw],
                         OP.mult, sem=False)
                    v_dv = c.tt(hb[:, :nw], sb["wb0"][:, :nw], sgb[:, n0:n0 + nw],
                                OP.mult)
                    ev_up[(mo, ni)] = (c.dv, v_dv)
                    c.wait("sync", c.dv, v_dv)
                    v_st = c.dma("sync",
                                 dram["hid"][128 * mo:128 * (mo + 1), n0:n0 + nw],
                                 hb[:, :nw], s_st)
                    st_hid[hidx] = v_st
            pe_done[(mo, br)] = c.pe.n

    c.fence_loads()
    ar.rewind(qmark)
    accb = ar.f32(T)
    hjb = [ar.f32(T), ar.f32(T)]
    habs = [ar.f32(T), ar.f32(T)]
    dv_acc = {}
    ac_abs = {}
    for j in range(43):
        r = j % 2
        if j >= 2:
            c.wait("sync", c.ac, ac_abs[j - 2])
        v_h = c.dma("sync", hjb[r], dram["hid"][128 * j:128 * (j + 1), :], s_h[r])
        c.wait("scalar", s_h[r], v_h)
        if j >= 2:
            c.wait("scalar", c.dv, dv_acc[j - 2])
        ac_abs[j] = c.act(habs[r], hjb[r], AF.Abs)
        c.wait("vector", c.ac, ac_abs[j])
        if j == 0:
            dv_acc[j] = c.copy(accb, habs[r])
        else:
            dv_acc[j] = c.tt(accb, accb, habs[r], OP.max)
    v_acc = c.dv.n
    c.wait("tensor", c.dv, v_acc)
    atok = sb["wb10"][:, :T // 128]
    for cb in range(T // 128):
        v_tp = c.transpose(ps["pb4"][:, :128], accb[:, 128 * cb:128 * (cb + 1)])
        c.wait("vector", c.pe, v_tp)
        v_rm = c.redmax(atok[:, cb:cb + 1], ps["pb4"][:, :128])
        c.wait("tensor", c.dv, v_rm)
    c.wait("vector", c.dv, c.dv.n)
    stok = sb["wb11"][:, :T // 128]
    rtok = sb["wb9"][:, :T // 128]
    c.tss(atok, atok, EPS, OP.max, sem=False)
    c.recip(stok, atok, sem=False)
    c.tss(stok, stok, 127.0, OP.mult, sem=False)
    v_dv = c.recip(rtok, stok)
    c.wait("sync", c.dv, v_dv)
    s2f = dram["s2flat"][0, :].rearrange("(cc p) -> p cc", p=128)
    r2f = dram["r2flat"][0, :].rearrange("(cc p) -> p cc", p=128)
    c.dma("sync", s2f, stok, s_st, slow=True)
    v_st = c.dma("sync", r2f, rtok, s_st, slow=True)
    c.wait("gpsimd", s_st, v_st)
    s_g2 = c.s("s4g2")
    for sl in range(8):
        c.dma("gpsimd", _rows(sb["s2b"][:], 16 * sl, 16),
              dram["s2flat"][0:1, :].broadcast_to([16, T]), s_g2)
        v_gd = c.dma("gpsimd", _rows(sb["r2b"][:], 16 * sl, 16),
                     dram["r2flat"][0:1, :].broadcast_to([16, T]), s_g2)
    c.wait("vector", s_g2, v_gd)

    vv = ar.f32(T)
    t0 = ar.f32(T)
    tb = ar.f32(T)
    db = ar.f32(T)
    hqt = ar.bf16(T)
    for j in range(43):
        r = j % 2
        if j >= 2:
            c.wait("sync", c.dv, dv_acc[j - 2])
        v_h = c.dma("sync", hjb[r], dram["hid"][128 * j:128 * (j + 1), :], s_h[r])
        c.wait("vector", s_h[r], v_h)
        c.tt(vv, hjb[r], sb["s2b"][:], OP.mult, sem=False)
        dv_acc[j] = ladder_round_clip(c, vv, t0, tb, db, hqt)
        c.wait("sync", c.dv, dv_acc[j])
        v_st = c.dma("sync", dram["hqb"][128 * j:128 * (j + 1), :], hqt, s_st)
        c.wait("vector", s_st, v_st)


def emit_s5(c):
    sb, ps, dram, ar = c.sb, c.ps, c.dram, c.ar
    wdh = ar.bf16(5504)
    wdl = ar.bf16(5504)
    hqf = [sb["s1b"][:, :T], ar.bf16(T)]
    dot = sb["wb0"][:, :512]
    s_w = c.s("s5w")
    s_hh = [c.s(f"s5h{i}") for i in range(2)]
    s_st = c.s("s5st")
    pe_prev = None
    dv_ev = None
    for mo in range(16):
        if pe_prev is not None:
            c.wait("sync", c.pe, pe_prev)
        c.dma("sync", wdh, dram["wqdh"][mo, :, :], s_w)
        v_w = c.dma("sync", wdl, dram["wqdl"][mo, :, :], s_w)
        c.wait("tensor", s_w, v_w)
        if dv_ev is not None:
            c.wait("tensor", c.dv, dv_ev)
        v_mm = {}
        for k in range(43):
            rr = k % 2
            if k >= 2:
                c.wait("sync", c.pe, v_mm[k - 2])
            v_h = c.dma("sync", hqf[rr],
                        dram["hqb"][128 * k:128 * (k + 1), :], s_hh[rr])
            c.wait("tensor", s_hh[rr], v_h)
            for tc in range(NTC):
                col = tc * TC
                pbk = ps[f"pb{4 + tc}"]
                c.mm(pbk[:, :TC], wdh[:, 128 * k:128 * (k + 1)],
                     hqf[rr][:, col:col + TC],
                     start=(k == 0), stop=False, sem=False)
                v_mm[k] = c.mm(pbk[:, :TC], wdl[:, 128 * k:128 * (k + 1)],
                               hqf[rr][:, col:col + TC], start=False,
                               stop=(k == 42), sem=(tc == NTC - 1))
            yield
        c.wait("vector", c.pe, v_mm[42])
        for tc in range(NTC):
            col = tc * TC
            pbk = ps[f"pb{4 + tc}"]
            dv_ev = c.tt(dot[:, :TC], pbk[:, :TC],
                         sb["r2b"][:, col:col + TC], OP.mult)
            c.wait("sync", c.dv, dv_ev)
            v_st = c.dma("sync", dram["dwn"][128 * mo:128 * (mo + 1),
                                             col:col + TC], dot[:, :TC], s_st)
            c.wait("vector", s_st, v_st)
        pe_prev = c.pe.n
        yield


def emit_s6(c):
    sb, ps, dram, ar = c.sb, c.ps, c.dram, c.ar
    ar.reset()
    Bb = ar.f32(T)
    Cb = ar.f32(T)
    dtb = [ar.f32(T), ar.f32(T)]
    ub = [ar.f32(T), ar.f32(T)]
    dab = dtb
    bsc = ub
    hsc = [ar.f32(T), ar.f32(T)]
    hcb = [ar.f32(T), ar.f32(T)]
    xcj = ar.f32(T)
    szj = ar.f32(T)
    ygf = c.sb["wb2"][:, :512]
    ygw = c.sb["wb3"][:, :512]
    psel = ar.f32(16 * 128)
    s_x = c.s("s6x")
    s_bc = c.s("s6bc")
    s_d = [c.s("s6dA"), c.s("s6dB")]
    s_u = [c.s("s6uA"), c.s("s6uB")]
    s_st = c.s("s6st")
    s_ps = c.s("s6ps")
    v_ps = c.dma("sync", psel, c.dram["psel_sb"][:], s_ps)
    c.wait("tensor", s_ps, v_ps)

    v_gd = None
    for sl in range(8):
        c.dma("gpsimd", _rows(Bb, 16 * sl, 16), dram["bcflat"][0:16, :], s_bc)
        v_gd = c.dma("gpsimd", _rows(Cb, 16 * sl, 16), dram["bcflat"][16:32, :],
                     s_bc)
    c.wait("vector", s_bc, v_gd)
    c.wait("scalar", s_bc, v_gd)
    c.wait("gpsimd", s_bc, v_gd)

    gd_dt, gd_u, dv_done, ac_done, pe_done, gp_done = {}, {}, {}, {}, {}, {}
    dv_evac = 0
    for j in range(32):
        c.wait("sync", c.dv, dv_evac)
        v_x = c.dma("sync", xcj, dram["xc"][128 * j:128 * (j + 1), :], s_x)
        v_s = c.dma("sync", szj, dram["sz"][128 * j:128 * (j + 1), :], s_x)
        for kk in range(16):
            c.run_filler(1)
            k = 16 * j + kk
            r = k % 2
            if k >= 2:
                c.wait("sync", c.ac, ac_done[k - 2])
                c.wait("sync", c.dv, dv_done[k - 2])
                c.wait("scalar", c.dv, dv_done[k - 2])
            for i in range(8):
                row = 8 * k + i
                c.dma("sync", _rows(dtb[r], 16 * i, 16),
                      dram["dt"][row:row + 1, :].broadcast_to([16, T]), s_d[r])
            gd_dt[k] = s_d[r].n
            for i in range(8):
                row = 8 * k + i
                c.dma("scalar", _rows(ub[r], 16 * i, 16),
                      dram["uu"][row:row + 1, :].broadcast_to([16, T]), s_u[r])
            gd_u[k] = s_u[r].n
            c.wait("scalar", s_d[r], gd_dt[k])
            if k >= 2:
                c.wait("scalar", c.dv, dv_done[k - 2])
            ac_done[k] = c.act(dab[r], dtb[r], AF.Exp,
                               scale=sb["aneg"][:, k:k + 1])
            c.wait("vector", s_u[r], gd_u[k])
            if k >= 2:
                c.wait("vector", c.gp, gp_done[k - 2])
            c.tt(bsc[r], ub[r], Bb, OP.mult, sem=False)
            c.wait("vector", c.ac, ac_done[k])
            dv_done[k] = c.scan(hsc[r], dab[r], bsc[r], 0.0)
            c.wait("gpsimd", c.dv, dv_done[k])
            if k >= 2:
                c.wait("gpsimd", c.pe, pe_done[k - 2])
            gp_done[k] = c.tt(hcb[r], hsc[r], Cb, OP.mult, eng="gpsimd")
            c.wait("tensor", c.gp, gp_done[k])
            if kk == 0:
                c.wait("tensor", c.dv, dv_evac)
            for n0, nw in spans(T):
                pe_done[k] = c.mm(ps[f"pb{(n0 // 512) % 4}"][:, :nw],
                                  psel[:, 128 * kk:128 * (kk + 1)],
                                  hcb[r][:, n0:n0 + nw],
                                  start=(kk == 0), stop=(kk == 15),
                                  sem=(kk == 15))
        c.wait("vector", c.pe, pe_done[16 * j + 15])
        c.wait("vector", s_x, v_s)
        for n0, nw in spans(T):
            c.stt(ygf[:, :nw], xcj[:, n0:n0 + nw], sb["dsk"][:, j:j + 1],
                  ps[f"pb{(n0 // 512) % 4}"][:, :nw], OP.mult, OP.add, sem=False)
            c.tt(ygw[:, :nw], ygf[:, :nw], szj[:, n0:n0 + nw], OP.mult, sem=False)
            yh = sb["bb0"][:, :nw]
            yl = sb["bb1"][:, :nw]
            c.copy(yh, ygw[:, :nw], sem=False)
            v_dv = c.tt(yl, ygw[:, :nw], yh, OP.subtract)
            c.wait("sync", c.dv, v_dv)
            c.dma("sync", dram["ygh"][128 * j:128 * (j + 1), n0:n0 + nw], yh,
                  s_st)
            v_st = c.dma("sync", dram["ygl"][128 * j:128 * (j + 1), n0:n0 + nw],
                         yl, s_st)
            c.wait("vector", s_st, v_st)
        dv_evac = c.dv.n


def emit_s7(c):
    sb, ps, dram, ar = c.sb, c.ps, c.dram, c.ar
    ar.reset()
    mark0 = ar.mark()
    yph = ar.bf16(32 * TC)
    ypl = ar.bf16(32 * TC)
    axb = ar.f32(16 * TC)
    opbh = [ar.bf16(4096), ar.bf16(4096)]
    opbl = [ar.bf16(4096), ar.bf16(4096)]
    mark1 = ar.mark()
    ar.rewind(mark0)
    sqb2 = ar.f32(16 * TC)
    ar.rewind(mark1)

    s_y = [c.s("s7yA"), c.s("s7yB")]
    s_w = c.s("s7w")
    s_xn = c.s("s7xn")
    s_st = c.s("s7st")
    sqb = sqb2
    for tc in range(NTC):
        col = tc * TC
        c.fence_loads()
        sy = s_y[tc % 2]
        for j in range(32):
            c.dma("sync", yph[:, TC * j:TC * (j + 1)],
                  dram["ygh"][128 * j:128 * (j + 1), col:col + TC], sy)
            c.dma("sync", ypl[:, TC * j:TC * (j + 1)],
                  dram["ygl"][128 * j:128 * (j + 1), col:col + TC], sy)
        v_ld = sy.n
        c.wait("tensor", sy, v_ld)
        pe_done, dv_ax = {}, {}
        s_w2 = [s_w, c.s("s7w2")]
        for mo in range(16):
            r = mo % 2
            if mo >= 2:
                c.wait("sync", c.pe, pe_done[mo - 2])
            c.dma("sync", opbh[r], dram["oph"][mo, :, :], s_w2[r])
            v_w = c.dma("sync", opbl[r], dram["opl"][mo, :, :], s_w2[r])
            if mo >= 1:
                c.wait("sync", c.dv, dv_ax[mo - 1])
            c.dma("sync", sb["bb6"][:, :TC],
                  dram["xnh"][128 * mo:128 * (mo + 1), col:col + TC], s_xn)
            v_xn = c.dma("sync", sb["bb7"][:, :TC],
                         dram["xnl"][128 * mo:128 * (mo + 1), col:col + TC],
                         s_xn)
            c.wait("tensor", s_w2[r], v_w)
            if mo >= 2:
                c.wait("tensor", c.dv, dv_ax[mo - 2])
            for n0, nw in spans(TC):
                pbk = ps[f"pb{mo % 2}"]
                v_mm = None
                for k in range(32):
                    yh = yph[:, TC * k + n0:TC * k + n0 + nw]
                    yl = ypl[:, TC * k + n0:TC * k + n0 + nw]
                    wh = opbh[r][:, 128 * k:128 * (k + 1)]
                    wl = opbl[r][:, 128 * k:128 * (k + 1)]
                    c.mm(pbk[:, :nw], wh, yh, start=(k == 0), stop=False,
                         sem=False)
                    c.mm(pbk[:, :nw], wh, yl, start=False, stop=False, sem=False)
                    v_mm = c.mm(pbk[:, :nw], wl, yh, start=False, stop=(k == 31))
                c.wait("vector", c.pe, v_mm)
                c.wait("vector", s_xn, v_xn)
                axt = axb[:, TC * mo + n0:TC * mo + n0 + nw]
                c.tt(sb["wb0"][:, :nw], sb["bb6"][:, n0:n0 + nw],
                     sb["bb7"][:, n0:n0 + nw], OP.add, sem=False)
                v_dv = c.tt(axt, sb["wb0"][:, :nw], pbk[:, :nw], OP.add)
                dv_ax[mo] = v_dv
                c.wait("sync", c.dv, v_dv)
                c.dma("sync", dram["anx"][128 * mo:128 * (mo + 1),
                                          col + n0:col + n0 + nw], axt, s_st)
            pe_done[mo] = c.pe.n

        pe_sq = {}
        ln_block(c, axb, 16, TC, "pb2", "pb3", sqb, c.dv, c.dv.n, pe_sq)
        m_b, var = sb["wb0"][:, :TC], sb["wb1"][:, :TC]
        r_b, tmp = sb["wb2"][:, :TC], sb["wb3"][:, :TC]
        ln_finish(c, "pb2", "pb3", TC, m_b, var, r_b, tmp)
        t1 = sb["wb7"][:, :TC]
        u2 = sb["wb8"][:, :TC]
        for i in range(16):
            xi = axb[:, TC * i:TC * (i + 1)]
            c.tt(t1, xi, m_b, OP.subtract, sem=False)
            v_dv = c.tt(t1, t1, r_b, OP.mult)
            c.wait("scalar", c.dv, v_dv)
            v_dv = c.stt(u2, t1, sb["ln2w"][:, i:i + 1],
                         sb["ln2b"][:, i:i + 1].broadcast_to([128, TC]),
                         OP.mult, OP.add)
            lh, ll = sb["bb0"][:, :TC], sb["bb1"][:, :TC]
            c.copy(lh, u2, sem=False)
            v_dv = c.tt(ll, u2, lh, OP.subtract)
            c.wait("sync", c.dv, v_dv)
            c.dma("sync", dram["ln2h"][128 * i:128 * (i + 1), col:col + TC], lh,
                  s_st)
            v_st = c.dma("sync",
                         dram["ln2l"][128 * i:128 * (i + 1), col:col + TC], ll,
                         s_st)
            c.wait("vector", s_st, v_st)


def emit_s8(c):
    sb, ps, dram, ar = c.sb, c.ps, c.dram, c.ar
    ar.reset()
    l2h = ar.bf16(16 * TC2)
    l2l = ar.bf16(16 * TC2)
    blkw = [ar.f32(2048), ar.f32(2048)]
    blkwh = [ar.bf16(2048), ar.bf16(2048)]
    blkwl = [ar.bf16(2048), ar.bf16(2048)]
    sgt = ar.f32(512)
    hgt = ar.f32(512)

    s_l2 = [c.s("s8lA"), c.s("s8lB")]
    s_w = [c.s("s8wA"), c.s("s8wB")]
    s_g = [c.s("s8gA"), c.s("s8gB")]
    s_st = c.s("s8st")

    import collections
    for tc2 in range(NT2):
        col = tc2 * TC2
        c.fence_loads()
        sl2 = s_l2[tc2 % 2]
        for i in range(16):
            c.dma("sync", l2h[:, TC2 * i:TC2 * (i + 1)],
                  dram["ln2h"][128 * i:128 * (i + 1), col:col + TC2], sl2)
            c.dma("sync", l2l[:, TC2 * i:TC2 * (i + 1)],
                  dram["ln2l"][128 * i:128 * (i + 1), col:col + TC2], sl2)
        v_l2 = sl2.n
        c.wait("tensor", sl2, v_l2)

        sub_lo = collections.defaultdict(lambda: None)
        pe_step, dv_hg = {}, {}

        def split(step, pe_guard):
            mo2, half = divmod(step, 2)
            blkid = mo2 if half == 0 else 32 + mo2
            r = step % 2
            if sub_lo[step - 2] is not None:
                c.wait("sync", c.dv, sub_lo[step - 2])
            v_w = c.dma("sync", blkw[r], dram["wt_ffproj"][blkid, :, :], s_w[r])
            c.wait("gpsimd", s_w[r], v_w)
            if pe_guard is not None:
                c.wait("gpsimd", c.pe, pe_guard)
            v_hi = c.dma("gpsimd", blkwh[r], blkw[r], s_g[r])
            c.wait("vector", s_g[r], v_hi)
            if pe_guard is not None:
                c.wait("vector", c.pe, pe_guard)
            sub_lo[step] = c.tt(blkwl[r], blkw[r], blkwh[r], OP.subtract)

        split(0, None)
        for step in range(64):
            c.run_filler(4)
            mo, half = divmod(step, 2)
            r = step % 2
            if step + 1 < 64:
                split(step + 1, pe_step.get(step - 1))
            c.wait("tensor", c.dv, sub_lo[step])
            if mo >= 1:
                c.wait("tensor", dv_hg.get((mo - 1, half), (c.dv, 0)))
            pboff = 0 if half == 0 else 2
            for n0, nw in spans(TC2):
                pbk = ps[f"pb{pboff + (n0 // 512) % 2}"]
                v_mm = None
                for k in range(16):
                    wh = blkwh[r][:, 128 * k:128 * (k + 1)]
                    wl = blkwl[r][:, 128 * k:128 * (k + 1)]
                    lh = l2h[:, TC2 * k + n0:TC2 * k + n0 + nw]
                    ll = l2l[:, TC2 * k + n0:TC2 * k + n0 + nw]
                    c.mm(pbk[:, :nw], wh, lh, start=(k == 0), stop=False,
                         sem=False)
                    c.mm(pbk[:, :nw], wh, ll, start=False, stop=False, sem=False)
                    v_mm = c.mm(pbk[:, :nw], wl, lh, start=False, stop=(k == 15))
                pe_n = getattr(c, "_s8pen", {})
                pe_n[(half, n0)] = v_mm
                c._s8pen = pe_n
            pe_step[step] = c.pe.n
            if half == 1:
                pe_n = c._s8pen
                v_dv = None
                for n0, nw in spans(TC2):
                    pa = ps[f"pb{(n0 // 512) % 2}"]
                    pg = ps[f"pb{2 + (n0 // 512) % 2}"]
                    c.wait("scalar", c.pe, pe_n[(1, n0)])
                    v_ac = c.act(sgt[:, :nw], pg[:, :nw], AF.Silu)
                    c.wait("vector", c.ac, v_ac)
                    c.wait("vector", c.pe, pe_n[(0, n0)])
                    c.tt(hgt[:, :nw], pa[:, :nw], sgt[:, :nw], OP.mult, sem=False)
                    hh, hl = sb["bb0"][:, :nw], sb["bb1"][:, :nw]
                    c.copy(hh, hgt[:, :nw], sem=False)
                    v_dv = c.tt(hl, hgt[:, :nw], hh, OP.subtract)
                    c.wait("sync", c.dv, v_dv)
                    c.dma("sync", dram["hgh"][128 * mo:128 * (mo + 1),
                                              col + n0:col + n0 + nw], hh, s_st)
                    v_st = c.dma("sync", dram["hgl"][128 * mo:128 * (mo + 1),
                                                     col + n0:col + n0 + nw], hl,
                                 s_st)
                    c.wait("vector", s_st, v_st)
                dv_hg[(mo, 0)] = (c.dv, v_dv)
                dv_hg[(mo, 1)] = (c.dv, v_dv)


def emit_s9(c):
    sb, ps, dram, ar = c.sb, c.ps, c.dram, c.ar
    ar.reset()
    hgh = ar.bf16(32 * TC)
    hgl = ar.bf16(32 * TC)
    fobh = [ar.bf16(4096), ar.bf16(4096)]
    fobl = [ar.bf16(4096), ar.bf16(4096)]
    xt = [ar.f32(TC), ar.f32(TC)]
    at = [ar.f32(TC), ar.f32(TC)]
    dwt = [ar.f32(TC), ar.f32(TC)]
    ot = [ar.f32(512), ar.f32(512)]

    s_hg = [c.s("s9hA"), c.s("s9hB")]
    s_w = [c.s("s9wA"), c.s("s9wB")]
    s_r = c.s("s9r")
    s_st = c.s("s9st")
    for tc in range(NTC):
        col = tc * TC
        c.fence_loads()
        shg = s_hg[tc % 2]
        for j in range(32):
            c.dma("sync", hgh[:, TC * j:TC * (j + 1)],
                  dram["hgh"][128 * j:128 * (j + 1), col:col + TC], shg)
            c.dma("sync", hgl[:, TC * j:TC * (j + 1)],
                  dram["hgl"][128 * j:128 * (j + 1), col:col + TC], shg)
        v_ld = shg.n
        c.wait("tensor", shg, v_ld)
        pe_done, dv_res, st_prev = {}, {}, {}
        for mo in range(16):
            r = mo % 2
            if mo >= 2:
                c.wait("sync", c.pe, pe_done[mo - 2])
            c.dma("sync", fobh[r], dram["foh"][mo, :, :], s_w[r])
            v_w = c.dma("sync", fobl[r], dram["fol"][mo, :, :], s_w[r])
            rows = slice(128 * mo, 128 * (mo + 1))
            if mo >= 2:
                c.wait("sync", c.dv, dv_res[mo - 2])
            c.dma("sync", xt[r], dram["x_f"][rows, col:col + TC], s_r)
            c.dma("sync", at[r], dram["anx"][rows, col:col + TC], s_r)
            v_res = c.dma("sync", dwt[r], dram["dwn"][rows, col:col + TC], s_r)
            c.wait("tensor", s_w[r], v_w)
            if mo >= 2:
                c.wait("tensor", c.dv, dv_res[mo - 2])
            for n0, nw in spans(TC):
                pbk = ps[f"pb{mo % 2}"]
                v_mm = None
                for k in range(32):
                    hh = hgh[:, TC * k + n0:TC * k + n0 + nw]
                    hl = hgl[:, TC * k + n0:TC * k + n0 + nw]
                    wh = fobh[r][:, 128 * k:128 * (k + 1)]
                    wl = fobl[r][:, 128 * k:128 * (k + 1)]
                    c.mm(pbk[:, :nw], wh, hh, start=(k == 0), stop=False,
                         sem=False)
                    c.mm(pbk[:, :nw], wh, hl, start=False, stop=False, sem=False)
                    v_mm = c.mm(pbk[:, :nw], wl, hh, start=False, stop=(k == 31))
                c.wait("vector", c.pe, v_mm)
                c.wait("vector", s_r, v_res)
                if mo >= 2:
                    c.wait("vector", s_st, st_prev[mo - 2])
                ob = ot[mo % 2]
                c.tt(ob[:, :nw], pbk[:, :nw], at[r][:, n0:n0 + nw], OP.add,
                     sem=False)
                c.tt(ob[:, :nw], ob[:, :nw], xt[r][:, n0:n0 + nw], OP.add,
                     sem=False)
                v_dv = c.tt(ob[:, :nw], ob[:, :nw], dwt[r][:, n0:n0 + nw], OP.add)
                c.wait("sync", c.dv, v_dv)
                c.dma("sync", dram["out_f"][rows, col + n0:col + n0 + nw],
                      ob[:, :nw], s_st)
                st_prev[mo] = s_st.n
            dv_res[mo] = c.dv.n
            pe_done[mo] = c.pe.n


def prep_inputs(inputs):
    g = _host_layouts(inputs)
    x = np.ascontiguousarray(np.asarray(inputs["L0_attn_x"], dtype=np.float32))
    xT = [np.ascontiguousarray(x[b].T[:, :T]) for b in range(NCORES)]
    return g, xT


def kernel(**inputs):
    g, xT = prep_inputs(inputs)
    nc = build_program()
    in_maps = []
    for b in range(NCORES):
        m = dict(g)
        m["x_f"] = xT[b]
        in_maps.append(m)
    res = run_bass_kernel_spmd(nc, in_maps, core_ids=list(range(NCORES)))
    out = np.stack([np.asarray(r["out_f"]).T for r in res.results])
    return out.astype(np.float32)
